# version 65
# baseline (speedup 1.0000x reference)
import math
import numpy as np
from contextlib import ExitStack

import concourse.bass as bass
import concourse.mybir as mybir
from concourse.bass_utils import run_bass_kernel_spmd

F32 = mybir.dt.float32
F32R = mybir.dt.float32r
BF16 = mybir.dt.bfloat16
F16 = mybir.dt.float16
I16 = mybir.dt.int16
AF = mybir.ActivationFunctionType
ALU = mybir.AluOpType
AXX = mybir.AxisListType.X

P = 128
D = 1024
FI = 2816
TCORE = 4096
NT = TCORE // P
KD = D // P
KI = FI // P
LN32 = math.log(32.0)
NCHUNKS = [(c * 512, min(512, FI - c * 512)) for c in range((FI + 511) // 512)]

SAFE_SYNC = False

RO_NAB2 = 0
RO_B = D
RO_B2 = 2 * D
RO_ONES = 3 * D
RO_SQ = 3 * D + P
ROWSF_W = 3 * D + 2 * P
CO_AB = 0
CO_AB2 = KD
CO_NAB2 = 2 * KD
CO_XLN = 3 * KD
CO_X512 = 4 * KD
CO_SQW = 5 * KD
COLSCR_W = 6 * KD


def build_core():
    nc = bass.Bass("TRN2", target_bir_lowering=False)
    xd = nc.dram_tensor("x", [TCORE, D], F32, kind="ExternalInput").ap()
    iwd = nc.dram_tensor("iw", [D, D], F32, kind="ExternalInput").ap()
    xisd = nc.dram_tensor("xis", [1, D], F32, kind="ExternalInput").ap()
    projd = nc.dram_tensor("proj", [D, D], F32, kind="ExternalInput").ap()
    wgs = nc.dram_tensor("wgs", [3 * P, D], F32, kind="ExternalInput").ap()
    wus = nc.dram_tensor("wus", [3 * P, D], F32, kind="ExternalInput").ap()
    wds = nc.dram_tensor("wds", [P, FI], F32, kind="ExternalInput").ap()
    gu_in = nc.dram_tensor("gu_in", [6 * P, D], F16)
    gu_out = nc.dram_tensor("gu_out", [48 * P, D], F16)
    wd_in = nc.dram_tensor("wd_in", [P, FI], F16)
    wd_out = nc.dram_tensor("wd_out", [8 * P, FI], F16)
    yd = nc.dram_tensor("y", [TCORE, D], F32, kind="ExternalOutput").ap()

    es = ExitStack()

    def sb(name, shape, dt_=F32):
        return es.enter_context(nc.sbuf_tensor(name, shape, dt_))

    def psum(name, shape, dt_=F32):
        return es.enter_context(nc.psum_tensor(name, shape, dt_))

    projST = sb("projST", [P, KD * D], F16)
    wT = sb("wT", [P, KD * D], F16)
    wgqT = sb("wgqT", [P, 22 * D], F16)
    wuqT = sb("wuqT", [P, 22 * D], F16)
    wdqT = sb("wdqT", [P, KD * FI], F16)
    xb0 = sb("xb0", [P, D])
    xb1 = sb("xb1", [P, D])
    xb2 = sb("xb2", [P, D])
    xbuf = [xb0, xb1, xb2]
    feats = sb("feats", [P, D], BF16)
    feats16 = feats[:].bitcast(F16)
    xq = sb("xq", [P, D], F16)
    hq = sb("hq", [P, FI], F16)
    qa2 = sb("qa2", [P, D], I16)
    qbi = sb("qbi", [P, D], I16)
    qd2 = sb("qd2", [P, D], I16)
    qc1 = sb("qc1", [P, D], BF16)
    qe1 = sb("qe1", [P, D], BF16)
    qm = sb("qm", [P, 64])
    qsc = sb("qsc", [P, 64])
    qrc = qm
    se = sb("se", [P, D])
    qa2b = qa2[:].bitcast(BF16)
    qa2f = qa2[:].bitcast(F16)
    qbib = qbi[:].bitcast(BF16)
    qd2b = qd2[:].bitcast(BF16)
    rowsF = sb("rowsF", [1, ROWSF_W], F16)
    colscr = sb("colscr", [P, COLSCR_W])
    ident_b = sb("ident_b", [P, P], BF16)
    ident_h = sb("ident_h", [P, P], F16)
    ident_f = sb("ident_f", [P, P])
    smalls = sb("smalls", [P, 18])
    one_col = smalls[:, 0:1]
    eps_col = smalls[:, 1:2]
    acc_a = smalls[:, 2:3]
    acc_c = smalls[:, 3:4]
    sqp_c = smalls[:, 4:5]
    LN_SLOTS = {
        1: tuple(smalls[:, 6 + i:7 + i] for i in range(6)),
        2: tuple(smalls[:, 12 + i:13 + i] for i in range(6)),
    }
    ps_dot = psum("ps_dot", [P, 512])
    ps_att = psum("ps_att", [P, 512])
    ps_tp = psum("ps_tp", [P, D], F16)
    ps_g = [psum("ps_g0", [P, 512]), psum("ps_g1", [P, 512])]
    ps_u = [psum("ps_u0", [P, 512]), psum("ps_u1", [P, 512])]
    ps_dn = psum("ps_dn", [P, 512])

    sems = {n: es.enter_context(nc.semaphore("s_" + n))
            for n in ["ld", "l1", "lw", "lx", "lg", "st", "s1", "s2",
                      "pe", "dv", "ac", "gp", "cc"]}
    cnt = {n: 0 for n in sems}
    ops = []
    hw = {"sync": {}, "vector": {}, "scalar": {}, "tensor": {}, "gpsimd": {}}
    OWN = {"vector": "dv", "scalar": "ac", "tensor": "pe", "gpsimd": "gp"}

    def _filter(engine, waits):
        out = []
        h = hw[engine]
        own = OWN.get(engine)
        for sn, val in waits:
            if val <= 0:
                continue
            if sn == own and cnt[sn] - 1 - val >= 2:
                continue
            if h.get(sn, 0) >= val:
                continue
            h[sn] = val
            out.append((sn, val))
        return tuple(out)

    def _add(engine, fn, waits, inc_sem, ret_last=False):
        waits = _filter(engine, waits)

        def f(e, w=waits):
            for sn, val in w:
                e.wait_ge(sems[sn], val)
            last = fn(e)
            if isinstance(last, list):
                last = last[-1] if last else None
            if inc_sem == "pe":
                last.then_inc(sems["pe"], 1)
            elif inc_sem is not None:
                if inc_sem != "gp" and last is not None and hasattr(last, "then_inc"):
                    last.then_inc(sems[inc_sem], 1)
                else:
                    e.drain().then_inc(sems[inc_sem], 1)
        ops.append((engine, f))

    def dma(dst, src, waits=(), ch="ld"):
        cnt[ch] += 16
        v = cnt[ch]
        if SAFE_SYNC:
            waits = tuple(waits) + ((ch, v - 16),)
        waits = _filter("sync", waits)

        def f(e, w=waits, d=dst, s=src, c=ch):
            for sn, val in w:
                e.wait_ge(sems[sn], val)
            e.dma_start(out=d, in_=s).then_inc(sems[c], 16)
        ops.append(("sync", f))
        return v

    def dma_out(dst, src, waits=(), ch="st"):
        cnt[ch] += 16
        v = cnt[ch]
        if SAFE_SYNC:
            waits = tuple(waits) + ((ch, v - 16),)
        waits = _filter("sync", waits)

        def f(e, w=waits, d=dst, s=src, c=ch):
            for sn, val in w:
                e.wait_ge(sems[sn], val)
            e.dma_start(out=d, in_=s).then_inc(sems[c], 16)
        ops.append(("sync", f))
        return v

    def dve(fn, waits=()):
        cnt["dv"] += 1
        _add("vector", fn, waits, "dv")
        return cnt["dv"]

    def act(fn, waits=()):
        cnt["ac"] += 1
        _add("scalar", fn, waits, "ac")
        return cnt["ac"]

    def pe(fn, waits=()):
        cnt["pe"] += 1
        _add("tensor", fn, waits, "pe", ret_last=True)
        return cnt["pe"]

    def gp(fn, waits=()):
        cnt["gp"] += 1
        _add("gpsimd", fn, waits, "gp")
        return cnt["gp"]

    last_tp_evac = [("ac", 0)]

    ps_tp_bf = ps_tp[:].bitcast(BF16)

    def transpose_group(src, src_offs, dst_view, pe_waits=(), bf=False):
        n = len(src_offs)
        ident = ident_b if bf else ident_h
        tp = ps_tp_bf if bf else ps_tp[:]

        def fpe(e):
            last = None
            for j, so in enumerate(src_offs):
                last = e.transpose(tp[:, j * P:(j + 1) * P],
                                   src[:, so:so + P], ident[:])
            return last
        vpe = pe(fpe, waits=tuple(pe_waits) + (last_tp_evac[0],))
        ve = act(lambda e: e.copy(dst_view, tp[:, :n * P]),
                 waits=(("pe", vpe),))
        last_tp_evac[0] = ("ac", ve)
        return ("ac", ve)

    def emit_quant_head(e, src_ap, w, qvp, nb_base=0, qsc_inv=1.0 / 6.0):
        t1 = (w // 3) // 16 * 16
        t2 = ((w - t1) // 2) // 16 * 16
        parts = [(0, t1), (t1, t2), (t1 + t2, w - t1 - t2)]
        pn = [(o // 16, cw // 16) for o, cw in parts]

        def allp(f):
            last = None
            for (o, cw), (nb0, nb) in zip(parts, pn):
                last = f(o, cw, nb0, nb)
            if SAFE_SYNC:
                last = e.drain()
            return last

        def blk(ap, o, cw):
            return ap[:, o:o + cw].rearrange("p (b k) -> p b k", k=16)

        nbb = nb_base
        allp(lambda o, cw, nb0, nb: e.tensor_reduce(
            qm[:, nbb + nb0:nbb + nb0 + nb], blk(src_ap, o, cw), AXX, ALU.max,
            apply_absolute_value=True))
        allp(lambda o, cw, nb0, nb: e.tensor_scalar(
            qsc[:, nbb + nb0:nbb + nb0 + nb], qm[:, nbb + nb0:nbb + nb0 + nb],
            qsc_inv, 1e-12, ALU.mult, ALU.max))
        allp(lambda o, cw, nb0, nb: e.reciprocal(
            qrc[:, nbb + nb0:nbb + nb0 + nb], qm[:, nbb + nb0:nbb + nb0 + nb]))
        allp(lambda o, cw, nb0, nb: e.tensor_scalar(
            qrc[:, nbb + nb0:nbb + nb0 + nb], qrc[:, nbb + nb0:nbb + nb0 + nb],
            6.0, 1e12, ALU.mult, ALU.min))
        return allp(lambda o, cw, nb0, nb: e.tensor_tensor(
            blk(qvp, o, cw), blk(src_ap, o, cw),
            qrc[:, nbb + nb0:nbb + nb0 + nb].unsqueeze(-1)
            .broadcast_to([P, nb, 16]), ALU.mult))

    def emit_quant_conv(e, w, qvp):
        e.activation(qa2[:, 0:w], qvp[:, 0:w], AF.Identity, scale=2.0)
        e.activation(qbi[:, 0:w], qvp[:, 0:w], AF.Identity)
        return e.activation(qd2[:, 0:w], qvp[:, 0:w], AF.Identity, scale=0.5)

    def emit_quant_tail(e, dst_ap, w, dve_conv=False, vbuf=None):
        nb = w // 16

        def blk(ap):
            return ap[:, 0:w].rearrange("p (b k) -> p b k", k=16)

        def s(x):
            if SAFE_SYNC:
                e.drain()
            return x
        if dve_conv:
            s(e.tensor_scalar(qa2[:, 0:w], vbuf[:, 0:w], 2.0, None, ALU.mult))
            s(e.tensor_copy(qbi[:, 0:w], vbuf[:, 0:w]))
            s(e.tensor_scalar(qd2[:, 0:w], vbuf[:, 0:w], 0.5, None, ALU.mult))
        s(e.tensor_scalar(qa2b[:, 0:w], qa2[:, 0:w], 4.0, -4.0, ALU.min, ALU.max))
        s(e.tensor_scalar(qc1[:, 0:w], qbi[:, 0:w], 4.0, -4.0, ALU.min, ALU.max))
        s(e.tensor_scalar(qbib[:, 0:w], qbi[:, 0:w], 2.0, -2.0, ALU.min, ALU.max))
        s(e.tensor_scalar(qe1[:, 0:w], qd2[:, 0:w], 3.0, -3.0, ALU.min, ALU.max))
        s(e.tensor_scalar(qd2b[:, 0:w], qd2[:, 0:w], 2.0, -2.0, ALU.min, ALU.max))
        s(e.tensor_tensor(qc1[:, 0:w], qc1[:, 0:w], qbib[:, 0:w], ALU.subtract))
        s(e.tensor_tensor(qe1[:, 0:w], qe1[:, 0:w], qd2b[:, 0:w], ALU.subtract))
        s(e.tensor_scalar(qa2b[:, 0:w], qa2b[:, 0:w], 0.5, None, ALU.mult))
        s(e.tensor_scalar(qe1[:, 0:w], qe1[:, 0:w], 2.0, None, ALU.mult))
        s(e.tensor_tensor(qc1[:, 0:w], qc1[:, 0:w], qa2b[:, 0:w], ALU.add))
        s(e.tensor_tensor(qc1[:, 0:w], qc1[:, 0:w], qe1[:, 0:w], ALU.add))
        return s(e.tensor_tensor(
            blk(dst_ap), blk(qc1),
            qsc[:, 0:nb].unsqueeze(-1).broadcast_to([P, nb, 16]), ALU.mult))

    def emit_quant_body(e, src_ap, dst_ap, w):
        emit_quant_head(e, src_ap, w, src_ap)
        return emit_quant_tail(e, dst_ap, w, dve_conv=True, vbuf=src_ap)

    def col_to_row(col_ap, dst_row_off, width, waits=()):
        v = 0
        for c0 in range(0, width // P, 4):
            cn = min(4, width // P - c0)

            def fpe(e, c0=c0, cn=cn):
                last = None
                for j in range(cn):
                    last = e.transpose(ps_dot[0:1, j * P:(j + 1) * P],
                                       col_ap[:, c0 + j:c0 + j + 1], ident_f[:])
                return last
            vpe = pe(fpe, waits=tuple(waits) + ((("dv", v),) if v else ()))
            v = dve(lambda e, c0=c0, cn=cn: e.tensor_copy(
                rowsF[0:1, dst_row_off + c0 * P:dst_row_off + (c0 + cn) * P],
                ps_dot[0:1, 0:cn * P]),
                waits=(("pe", vpe),))
        return v

    v_gp0 = gp(lambda e: [
        e.memset(ident_b[:], 0.0),
        e.memset(ident_f[:], 0.0),
        e.memset(ident_h[:], 0.0),
        e.drain(),
        e.affine_select(out=ident_b[:], in_=ident_b[:], compare_op=ALU.not_equal,
                        fill=1.0, base=0, pattern=[[-1, P]], channel_multiplier=1),
        e.affine_select(out=ident_f[:], in_=ident_f[:], compare_op=ALU.not_equal,
                        fill=1.0, base=0, pattern=[[-1, P]], channel_multiplier=1),
        e.affine_select(out=ident_h[:], in_=ident_h[:], compare_op=ALU.not_equal,
                        fill=1.0, base=0, pattern=[[-1, P]], channel_multiplier=1),
        e.memset(one_col, 1.0),
        e.memset(eps_col, 1e-5),
        e.memset(rowsF[0:1, RO_ONES:RO_ONES + P], 1.0),
    ])
    v_ones = dve(lambda e: e.memset(smalls[:, 5:6], 0.0),
                 waits=(("gp", v_gp0),))

    v_xis = dma(xb1[0:1, 0:D], xisd[:])

    def f_abcol(e):
        last = None
        for j in range(KD):
            last = e.transpose(ps_g[0][:, j:j + 1], xb1[0:1, j * P:(j + 1) * P],
                               ident_f[0:1, 0:1])
        return last
    v_abpe = pe(f_abcol, waits=(("ld", v_xis), ("gp", v_gp0)))
    v_abcol = dve(lambda e: e.tensor_copy(colscr[:, CO_AB:CO_AB + KD],
                                          ps_g[0][:, 0:KD]),
                  waits=(("pe", v_abpe),))

    v_cols = dve(lambda e: [
        e.tensor_tensor(colscr[:, CO_AB2:CO_AB2 + KD], colscr[:, CO_AB:CO_AB + KD],
                        colscr[:, CO_AB:CO_AB + KD], ALU.mult),
        e.drain(),
        e.tensor_scalar(colscr[:, CO_NAB2:CO_NAB2 + KD],
                        colscr[:, CO_AB2:CO_AB2 + KD], -1.0, None, ALU.mult),
        e.tensor_scalar(colscr[:, CO_XLN:CO_XLN + KD],
                        colscr[:, CO_AB:CO_AB + KD], -LN32, None, ALU.add),
        e.drain(),
        e.scalar_tensor_tensor(colscr[:, CO_X512:CO_X512 + KD],
                               colscr[:, CO_AB2:CO_AB2 + KD], -512.0,
                               colscr[:, CO_XLN:CO_XLN + KD], ALU.mult, ALU.add),
    ], waits=(("dv", v_abcol),))

    v_nab2row = col_to_row(colscr[:, CO_NAB2:CO_NAB2 + KD], RO_NAB2, D,
                           waits=(("dv", v_cols),))

    projST_3d = projST[:].rearrange("p (k q) -> p k q", q=D)
    iwpark_3d = wgqT[:].rearrange("p (j q) -> p j q", q=D)
    wgqT_j3 = wgqT[:].rearrange("p (j q) -> p j q", q=D)
    wuqT_j3 = wuqT[:].rearrange("p (j q) -> p j q", q=D)
    wdqT_j3 = wdqT[:].rearrange("p (j q) -> p j q", q=FI)

    se16 = se[:].bitcast(F16)
    stbuf = [xb0, xb1]
    prev_use = {0: ("dv", v_ones), 1: ("pe", v_abpe)}
    item_idx = [0]
    prev_contrib = [0]

    def stage_load(src_ap, pw=D):
        p = item_idx[0] % 2
        item_idx[0] += 1
        buf = stbuf[p]
        ch = "ld" if p == 0 else "l1"
        v_l = dma(buf[:, :pw], src_ap, waits=(prev_use[p],), ch=ch)
        return buf, (ch, v_l), p

    def shard_item(srcrows, contrib_rows):
        buf, w_l, p = stage_load(srcrows)
        vq = dve(lambda e, b=buf: emit_quant_body(e, b[:], xq[:], D),
                 waits=(w_l, last_tp_evac[0]))
        prev_use[p] = ("dv", vq)
        wev = transpose_group(xq, [k * P for k in range(KD)], se16[:, 0:D],
                              pe_waits=(("dv", vq), ("st", prev_contrib[0])))
        vc = dma_out(contrib_rows, se16[:, 0:D], waits=(wev,))
        prev_contrib[0] = vc
        return vc

    for i in range(3):
        v_contrib_gu = shard_item(wus[i * P:(i + 1) * P, :],
                                  gu_in[i * P:(i + 1) * P, :])
    for i in range(3):
        v_contrib_gu = shard_item(wgs[i * P:(i + 1) * P, :],
                                  gu_in[(3 + i) * P:(4 + i) * P, :])
    for pc in range(3):
        p0 = pc * D
        pw = min(D, FI - p0)
        buf, w_l, p = stage_load(wds[:, p0:p0 + pw], pw)
        vq = dve(lambda e, b=buf, c=pw: emit_quant_body(e, b[:, 0:c],
                                                        xq[:, 0:c], c),
                 waits=(w_l, last_tp_evac[0]))
        prev_use[p] = ("dv", vq)
        wev = transpose_group(xq, [c * P for c in range(pw // P)],
                              hq[:, p0:p0 + pw], pe_waits=(("dv", vq),))
    v_contrib_wd = dma_out(wd_in.ap(), hq[:, 0:FI], waits=(wev,))

    v_cc = gp(lambda e: [
        e.wait_ge(sems["st"], v_contrib_gu),
        e.collective_compute("AllGather", ALU.bypass,
                             replica_groups=[list(range(8))],
                             ins=[gu_in.ap().opt()], outs=[gu_out.ap().opt()]
                             ).then_inc(sems["cc"], 1),
        e.wait_ge(sems["st"], v_contrib_wd),
        e.collective_compute("AllGather", ALU.bypass,
                             replica_groups=[list(range(8))],
                             ins=[wd_in.ap().opt()], outs=[wd_out.ap().opt()]
                             ).then_inc(sems["cc"], 1),
        e.wait_ge(sems["cc"], 2),
    ] + [e.dma_start(out=wdqT[:, j * FI:(j + 1) * FI],
                     in_=wd_out.ap()[j * P:(j + 1) * P, :]
                     ).then_inc(sems["lg"], 16) for j in range(KD)]
      + [e.wait_ge(sems["ac"], v_wT_exp)]
      + [e.dma_start(out=wgqT[:, j * D:(j + 1) * D],
                     in_=gu_out.ap()[((j // 3) * 6 + 3 + (j % 3)) * P:
                                     ((j // 3) * 6 + 4 + (j % 3)) * P, :]
                     ).then_inc(sems["lg"], 16) for j in range(22)])

    for j in range(KD):
        buf, w_l, p = stage_load(projd[j * P:(j + 1) * P, :])
        v_s = dve(lambda e, b=buf, jj=j: e.tensor_scalar(
            xq[:], b[:], colscr[:, CO_AB + jj:CO_AB + jj + 1], None, ALU.mult),
            waits=(w_l, ("dv", v_cols), last_tp_evac[0]))
        prev_use[p] = ("dv", v_s)
        transpose_group(xq, [k * P for k in range(KD)],
                        projST_3d[:, :, j * P:(j + 1) * P], pe_waits=(("dv", v_s),))

    prev_sq = 0
    for j in range(KD):
        buf, w_l, p = stage_load(iwd[j * P:(j + 1) * P, :])
        v_sq = act(lambda e, b=buf: [e.activation(se[:], b[:], AF.Square,
                                                  accum_out=acc_a), e.drain()],
                   waits=(w_l, ("dv", prev_sq),
                          ("st", v_contrib_gu)))
        prev_sq = dve(lambda e, jj=j: e.tensor_scalar(
            colscr[:, CO_SQW + jj:CO_SQW + jj + 1], acc_a, 0.5, None, ALU.mult),
            waits=(("ac", v_sq),))
        v_c = dve(lambda e, b=buf: e.tensor_copy(xq[:], b[:]),
                  waits=(("ac", v_sq), last_tp_evac[0]))
        prev_use[p] = ("dv", v_c)
        transpose_group(xq, [k * P for k in range(KD)],
                        iwpark_3d[:, 0:KD, j * P:(j + 1) * P], pe_waits=(("dv", v_c),))
    prev_consume = max(prev_use[0], prev_use[1], key=lambda w: w[1])

    v_sqwrow = col_to_row(colscr[:, CO_SQW:CO_SQW + KD], RO_B, D,
                          waits=(("dv", prev_sq),))

    v_wT_exp = 0
    for i in range(KD):
        for oc in range(2):
            def f_mm(e, ii=i, occ=oc):
                for k in range(KD):
                    e.matmul(ps_dot[:, 0:512],
                             projST_3d[:, k, ii * P:(ii + 1) * P],
                             iwpark_3d[:, k, occ * 512:(occ + 1) * 512],
                             start=(k == 0), stop=False)
                return e.matmul(ps_dot[:, 0:512],
                                rowsF[0:1, RO_NAB2 + ii * P:RO_NAB2 + (ii + 1) * P],
                                rowsF[0:1, RO_B + occ * 512:RO_B + (occ + 1) * 512],
                                start=False, stop=True)
            v_mm = pe(f_mm, waits=(("dv", max(v_sqwrow, v_nab2row)),
                                   prev_consume, last_tp_evac[0],
                                   ("ac", v_wT_exp)))
            v_wT_exp = act(lambda e, ii=i, occ=oc: e.activation(
                wT[:, ii * D + occ * 512: ii * D + (occ + 1) * 512],
                ps_dot[:, 0:512], AF.Exp,
                bias=colscr[:, CO_XLN + ii:CO_XLN + ii + 1]),
                waits=(("pe", v_mm),))

    v_x512row = 0
    for c0 in (0, 4):
        def f_x512pe(e, c0=c0):
            last = None
            for j in range(4):
                last = e.transpose(ps_dot[0:1, j * P:(j + 1) * P],
                                   colscr[:, CO_X512 + c0 + j:CO_X512 + c0 + j + 1],
                                   ident_f[:])
            return last
        v_x512pe = pe(f_x512pe, waits=(("ac", v_wT_exp), ("dv", v_sqwrow),
                                       ("dv", v_x512row)))
        v_x512row = dve(lambda e, c0=c0: [
            e.tensor_copy(rowsF[0:1, RO_B + c0 * P:RO_B + (c0 + 4) * P],
                          ps_dot[0:1, 0:512]),
            e.drain(),
            e.tensor_tensor(rowsF[0:1, RO_B2 + c0 * P:RO_B2 + (c0 + 4) * P],
                            ps_dot[0:1, 0:512],
                            rowsF[0:1, RO_B + c0 * P:RO_B + (c0 + 4) * P],
                            ALU.subtract),
        ], waits=(("pe", v_x512pe),))

    v_wload = 0
    for j in range(22):
        r, i = divmod(j, 3)
        v_wload = dma(wuqT[:, j * D:(j + 1) * D],
                      gu_out.ap()[(r * 6 + i) * P:(r * 6 + i + 1) * P, :],
                      waits=(("cc", 1),), ch="lw")


    st = [dict() for _ in range(NT + 1)]
    st[0]["att1"] = 0
    LCH = ["ld", "l1", "lx"]
    SCH = ["st", "s1", "s2"]
    vx = [0] * (NT + 3)
    vx[0] = dma(xb0[:], xd[0:P, :], waits=(("dv", v_x512row),), ch=LCH[0])
    vx[1] = dma(xb1[:], xd[P:2 * P, :], waits=(("dv", v_x512row),), ch=LCH[1])
    vx[2] = dma(xb2[:], xd[2 * P:3 * P, :], waits=(), ch=LCH[2])

    def g(t, key, default=0):
        return st[t][key] if t >= 0 and key in st[t] else default

    def emit_A_head(t):
        xb = xbuf[t % 3]
        s = st[t]
        s["cast"] = dve(lambda e, b=xb: e.tensor_copy(feats16[:], b[:]),
                        waits=((LCH[t % 3], vx[t]), ("pe", g(t - 1, "att1"))))
        v_sqp = dve(lambda e, b=xb: [
            e.scalar_tensor_tensor(hq[:, 0:D], b[:], 1.0, b[:], ALU.mult,
                                   ALU.mult, accum_out=acc_a),
            e.drain(),
            e.tensor_scalar(sqp_c, acc_a, 0.5, -512.0, ALU.mult, ALU.add),
        ], waits=((LCH[t % 3], vx[t]),))
        s["sqp"] = v_sqp
        v_sqT = pe(lambda e: e.transpose(ps_att[0:1, 0:P], sqp_c, ident_f[:]),
                   waits=(("dv", max(v_sqp, g(t - 1, "res"))),))
        s["corr"] = dve(lambda e: e.tensor_copy(rowsF[0:1, RO_SQ:RO_SQ + P],
                                                ps_att[0:1, 0:P]),
                        waits=(("pe", v_sqT), ("dv", g(t - 1, "corr"))))
        s["xT"] = transpose_group(feats16, [k * P for k in range(KD)],
                                  feats16[:, 0:D], pe_waits=(("dv", s["cast"]),))

    def emit_dot_h(t, b):
        s = st[t]

        def f_dot(e, bb=b, ps=(ps_dot if b == 0 else ps_att)):
            for k in range(KD):
                e.matmul(ps[:, 0:512],
                         feats16[:, k * P:(k + 1) * P],
                         projST_3d[:, k, bb * 512:(bb + 1) * 512],
                         start=(k == 0), stop=False)
            e.matmul(ps[:, 0:512],
                     rowsF[0:1, RO_SQ:RO_SQ + P],
                     rowsF[0:1, RO_NAB2 + bb * 512:RO_NAB2 + (bb + 1) * 512],
                     start=False, stop=False)
            e.matmul(ps[:, 0:512],
                     rowsF[0:1, RO_ONES:RO_ONES + P],
                     rowsF[0:1, RO_B + bb * 512:RO_B + (bb + 1) * 512],
                     start=False, stop=False)
            return e.matmul(ps[:, 0:512],
                            rowsF[0:1, RO_ONES:RO_ONES + P],
                            rowsF[0:1, RO_B2 + bb * 512:RO_B2 + (bb + 1) * 512],
                            start=False, stop=True)
        w = [s["xT"], ("dv", s["corr"])]
        if t == 0:
            w.append(("dv", v_x512row))
        if b == 0:
            w.append(("ac", g(t - 1, "exp", v_wT_exp)))
        else:
            w.append(("dv", g(t - 1, "res")))
        s[f"dot{b}"] = pe(f_dot, waits=tuple(w))
        if b == 1:
            s["exp"] = act(lambda e: [
                e.activation(feats[:, 0:512], ps_dot[:, 0:512], AF.Exp),
                e.activation(feats[:, 512:D], ps_att[:, 0:512], AF.Exp),
            ], waits=(("pe", s["dot1"]),))
            s["fT"] = transpose_group(feats, [i * P for i in range(KD)],
                                      feats[:, 0:D],
                                      pe_waits=(("ac", s["exp"]),), bf=True)

    def emit_att_h(t, b):
        s = st[t]

        def f_att(e, bb=b):
            last = None
            for i in range(KD):
                last = e.matmul(ps_att[:, 0:512],
                                feats[:, i * P:(i + 1) * P],
                                wT[:, i * D + bb * 512: i * D + (bb + 1) * 512],
                                start=(i == 0), stop=(i == KD - 1))
            return last
        xb = xbuf[t % 3]
        if b == 0:
            v = pe(f_att, waits=(s["fT"], ("ac", s["exp"])))
            s["add0"] = dve(lambda e, bx=xb: e.tensor_tensor(
                bx[:, 0:512], bx[:, 0:512], ps_att[:, 0:512], ALU.add),
                waits=(("pe", v),))
        else:
            v = pe(f_att, waits=(s["fT"], ("dv", s["add0"])))
            s["att1"] = v
            s["res"] = dve(lambda e, bx=xb: [
                e.tensor_tensor(bx[:, 512:D], bx[:, 512:D], ps_att[:, 0:512],
                                ALU.add),
                e.tensor_scalar(qa2f[:, 0:D], bx[:], 1.0, None, ALU.mult,
                                ALU.add, accum_out=LN_SLOTS[1][1]),
            ], waits=(("pe", v),))

    def emit_ln(which, v_ready, xbt):
        s_sq, s_sum, s_mean, s_var, s_rstd, s_nmb = LN_SLOTS[which]
        junk = se[:] if which == 1 else hq[:, 0:D]
        v_s2 = act(lambda e, j=junk: [e.activation(j, xbt[:], AF.Square,
                                                   accum_out=s_sq), e.drain()],
                   waits=(("dv", v_ready),))
        if which == 2:
            cur_ln2sq[0] = v_s2
        v_var = dve(lambda e: [
            e.scalar_tensor_tensor(s_var, s_sum, 1.0 / D, s_sum, ALU.mult, ALU.mult),
            e.tensor_scalar(s_mean, s_sum, -1.0 / D, None, ALU.mult),
            e.drain(),
            e.tensor_tensor(s_var, s_sq, s_var, ALU.subtract),
        ], waits=(("ac", v_s2),))
        return act(lambda e: [
            e.activation(s_rstd, s_var, AF.Ln, bias=eps_col, scale=1.0 / D),
            e.drain(),
            e.activation(s_rstd, s_rstd, AF.Exp, scale=-0.5),
            e.drain(),
            e.activation(s_nmb, s_mean, AF.Identity, scale=s_rstd),
            e.drain(),
            e.activation(xbt[:], xbt[:], AF.Identity, bias=s_nmb, scale=s_rstd),
        ], waits=(("dv", v_var),))

    cur_ln2sq = [0]

    def emit_ln1(t):
        s = st[t]
        s["x1"] = emit_ln(1, max(s["res"], g(t - 1, "hq2")), xbuf[t % 3])

    def emit_xq_head(t):
        xb = xbuf[t % 3]
        s = st[t]
        s["xqh"] = dve(lambda e, b=xb: emit_quant_head(e, b[:], D, se),
                       waits=(("ac", s["x1"]),))

    def emit_xq_tail(t):
        s = st[t]
        s["xq"] = dve(lambda e: emit_quant_tail(e, xq[:], D, dve_conv=True,
                                                vbuf=se),
                      waits=(("pe", g(t - 1, "mmv5")),))

    def emit_xqT(t):
        s = st[t]
        s["xqT"] = transpose_group(xq, [k * P for k in range(KD)], xq[:, 0:D],
                                   pe_waits=(("dv", s["xq"]),))

    CHUNKS = [(h * 512, min(512, FI - h * 512)) for h in range(6)]

    def emit_chunk(t, h):
        s = st[t]
        off, cw = CHUNKS[h]
        p = h % 2
        seo = p * 512
        nj = cw // P
        j0 = h * 4

        def f_mm_g(e, cw=cw, p=p, j0=j0, nj=nj):
            last = None
            for k in range(KD):
                last = e.matmul(ps_g[p][:, 0:cw], xq[:, k * P:(k + 1) * P],
                                wgqT_j3[:, j0:j0 + nj, k * P:(k + 1) * P],
                                start=(k == 0), stop=(k == KD - 1))
            return last

        def f_mm_u(e, cw=cw, p=p, j0=j0, nj=nj):
            last = None
            for k in range(KD):
                last = e.matmul(ps_u[p][:, 0:cw], xq[:, k * P:(k + 1) * P],
                                wuqT_j3[:, j0:j0 + nj, k * P:(k + 1) * P],
                                start=(k == 0), stop=(k == KD - 1))
            return last
        w = []
        wu = []
        if h == 0:
            w.append(s["xqT"])
        if h < 2:
            w.append(("dv", g(t - 1, "mulg%d" % (4 + h))))
            wu.append(("dv", g(t - 1, "mul%d" % (4 + h))))
            if t == 0:
                w.append(("lw", v_wload))
                w.append(("lg", 16 * (KD + 22)))
        else:
            w.append(("dv", s["mulg%d" % (h - 2)]))
            wu.append(("dv", s["mul%d" % (h - 2)]))
        v_mm = pe(f_mm_g, waits=tuple(w))
        v_mmu = pe(f_mm_u, waits=tuple(wu))
        s["mmv%d" % h] = v_mmu

        pair_prev = h // 2 - 1
        v_sig = act(lambda e, seo=seo, cw=cw, p=p: e.activation(
            se[:, seo:seo + cw], ps_g[p][:, 0:cw], AF.Tanh, scale=0.5),
            waits=(("pe", v_mm),
                   ("dv", s["hq%d" % pair_prev] if pair_prev >= 0
                    else s["xq"])))

        s["mulg%d" % h] = dve(lambda e, seo=seo, cw=cw, p=p:
                              e.scalar_tensor_tensor(
                                  se[:, seo:seo + cw], se[:, seo:seo + cw],
                                  1.0, ps_g[p][:, 0:cw], ALU.add, ALU.mult),
                              waits=(("ac", v_sig),))
        s["mul%d" % h] = dve(lambda e, seo=seo, cw=cw, p=p: e.tensor_tensor(
            se[:, seo:seo + cw], se[:, seo:seo + cw], ps_u[p][:, 0:cw],
            ALU.mult), waits=(("pe", v_mmu),))

    def emit_pair_qhA(t, pi):
        s = st[t]
        s["qa%d" % pi] = dve(lambda e: emit_quant_head(
            e, se[:, 0:512], 512, se, qsc_inv=1.0 / 12.0),
            waits=(("dv", s["mul%d" % (2 * pi)]),))

    def emit_pair_quant(t, pi):
        s = st[t]
        pw = 1024 if pi < 2 else 768
        s["qh%d" % pi] = dve(lambda e, c=pw - 512: emit_quant_head(
            e, se[:, 512:512 + c], c, se[:, 512:], nb_base=32,
            qsc_inv=1.0 / 12.0),
            waits=(("dv", s["mul%d" % (2 * pi + 1)]),))
        s["hq%d" % pi] = dve(lambda e, o=1024 * pi, c=pw: emit_quant_tail(
            e, hq[:, o:o + c], c, dve_conv=True, vbuf=se),
            waits=(("pe", g(t - 1, "dno1") if pi == 0 else 0),
                   ("ac", g(t - 1, "ln2sq") if pi == 0 else 0),
                   ("lw", v_wload if (t == 0 and pi == 0) else 0),
                   ("lg", 16 * (KD + 22) if (t == 0 and pi == 0) else 0)))

    DOWN_GROUPS = [list(range(0, 8)), list(range(8, 16)), list(range(16, 22))]

    def emit_Thq_down(t, pgrp):
        s = st[t]
        cs = DOWN_GROUPS[pgrp]
        wev = transpose_group(hq, [c * P for c in cs],
                              hq[:, cs[0] * P:(cs[-1] + 1) * P],
                              pe_waits=(("dv", s["hq%d" % pgrp]),))

        def f_d(e, cs=tuple(cs), first=(pgrp == 0), lastg=(pgrp == 2)):
            last = None
            for i, c in enumerate(cs):
                last = e.matmul(ps_dn[:, 0:512], hq[:, c * P:(c + 1) * P],
                                wdqT_j3[:, 0:4, c * P:(c + 1) * P],
                                start=(first and i == 0),
                                stop=(lastg and i == len(cs) - 1))
            return last
        w = [wev]
        if pgrp == 0:
            w.append(("dv", g(t - 1, "res2")))
        v = pe(f_d, waits=tuple(w))
        if pgrp == 2:
            s["dno0"] = v
            s["ThqL"] = wev

    def emit_down_o1a(t):
        s = st[t]
        xb = xbuf[t % 3]
        s["res2h0"] = dve(lambda e, b=xb: e.tensor_tensor(
            b[:, 0:512], b[:, 0:512], ps_dn[:, 0:512], ALU.add),
            waits=(("pe", s["dno0"]),))

        def f_d1(e):
            last = None
            for c in range(KI):
                last = e.matmul(ps_dn[:, 0:512], hq[:, c * P:(c + 1) * P],
                                wdqT_j3[:, 4:8, c * P:(c + 1) * P],
                                start=(c == 0), stop=(c == KI - 1))
            return last
        s["dno1"] = pe(f_d1, waits=(("dv", s["res2h0"]), s["ThqL"]))

    def emit_down_o1b(t):
        s = st[t]
        xb = xbuf[t % 3]
        s["res2"] = dve(lambda e, b=xb: [
            e.tensor_tensor(b[:, 512:D], b[:, 512:D], ps_dn[:, 0:512], ALU.add),
            e.tensor_scalar(qa2f[:, 0:D], b[:], 1.0, None, ALU.mult, ALU.add,
                            accum_out=LN_SLOTS[2][1]),
        ], waits=(("pe", s["dno1"]),))

    def emit_ln2_store(t):
        xb = xbuf[t % 3]
        s = st[t]
        v_y = emit_ln(2, s["res2"], xb)
        s["ln2sq"] = cur_ln2sq[0]
        vs = dma_out(yd[t * P:(t + 1) * P, :], xb[:], waits=(("ac", v_y),),
                     ch=SCH[t % 3])
        if t + 3 < NT:
            vx[t + 3] = dma(xbuf[t % 3][:], xd[(t + 3) * P:(t + 4) * P, :],
                            waits=((SCH[t % 3], vs),), ch=LCH[t % 3])

    emit_A_head(0)
    emit_dot_h(0, 0)
    emit_dot_h(0, 1)
    emit_att_h(0, 0)
    emit_att_h(0, 1)
    emit_ln1(0)
    emit_xq_head(0)
    emit_xq_tail(0)
    for t in range(NT):
        emit_xqT(t)
        if t + 1 < NT:
            emit_A_head(t + 1)
        emit_chunk(t, 0)
        emit_pair_qhA(t, 0)
        emit_chunk(t, 1)
        if t + 1 < NT:
            emit_dot_h(t + 1, 0)
        emit_pair_quant(t, 0)
        emit_chunk(t, 2)
        emit_pair_qhA(t, 1)
        if t + 1 < NT:
            emit_dot_h(t + 1, 1)
        emit_chunk(t, 3)
        emit_pair_quant(t, 1)
        emit_Thq_down(t, 0)
        if t + 1 < NT:
            emit_att_h(t + 1, 0)
        emit_chunk(t, 4)
        emit_pair_qhA(t, 2)
        emit_chunk(t, 5)
        if t + 1 < NT:
            emit_att_h(t + 1, 1)
        emit_pair_quant(t, 2)
        emit_Thq_down(t, 1)
        if t + 1 < NT:
            emit_ln1(t + 1)
        emit_Thq_down(t, 2)
        if t + 1 < NT:
            emit_xq_head(t + 1)
        emit_down_o1a(t)
        if t + 1 < NT:
            emit_xq_tail(t + 1)
        emit_down_o1b(t)
        emit_ln2_store(t)

    tot = {c: cnt[c] for c in SCH}
    ops.append(("sync", lambda e: [e.wait_ge(sems[c], tot[c])
                                   for c in SCH if tot[c]]))

    block = es.enter_context(nc.Block())

    def replay(name):
        def f(e):
            for en, fn in ops:
                if en == name:
                    fn(e)
        return f
    block.sync(replay("sync"))
    block.vector(replay("vector"))
    block.scalar(replay("scalar"))
    block.tensor(replay("tensor"))
    block.gpsimd(replay("gpsimd"))
    es.close()
    return nc


_NC_CACHE = None


def kernel(L0_attn_x, L0_attn_input_weights, L0_attn_xis, L0_attn_proj_matrix,
           L0_ffn_w_gate, L0_ffn_w_up, L0_ffn_w_down):
    global _NC_CACHE
    x = np.ascontiguousarray(np.asarray(L0_attn_x, dtype=np.float32)).reshape(8 * TCORE, D)
    iw = np.ascontiguousarray(np.asarray(L0_attn_input_weights, dtype=np.float32))
    xis = np.ascontiguousarray(np.asarray(L0_attn_xis, dtype=np.float32)).reshape(1, D)
    proj = np.ascontiguousarray(np.asarray(L0_attn_proj_matrix, dtype=np.float32))
    wg = np.zeros((24 * P, D), dtype=np.float32)
    wg[:FI] = np.asarray(L0_ffn_w_gate, dtype=np.float32)
    wu = np.zeros((24 * P, D), dtype=np.float32)
    wu[:FI] = np.asarray(L0_ffn_w_up, dtype=np.float32)
    wd = np.ascontiguousarray(np.asarray(L0_ffn_w_down, dtype=np.float32))

    if _NC_CACHE is None:
        _NC_CACHE = build_core()
    nc = _NC_CACHE
    in_maps = []
    for c in range(8):
        in_maps.append({
            "x": x[c * TCORE:(c + 1) * TCORE],
            "iw": iw, "xis": xis, "proj": proj,
            "wgs": np.ascontiguousarray(wg[c * 3 * P:(c + 1) * 3 * P]),
            "wus": np.ascontiguousarray(wu[c * 3 * P:(c + 1) * 3 * P]),
            "wds": np.ascontiguousarray(wd[c * P:(c + 1) * P]),
        })
    res = run_bass_kernel_spmd(nc, in_maps, core_ids=list(range(8)))
    y = np.concatenate([res.results[c]["y"] for c in range(8)], axis=0)
    return y.reshape(8, TCORE, D).astype(np.float32)


if __name__ == "__main__":
    rng = np.random.RandomState(0)
    ins = {
        "L0_attn_x": rng.randn(8, TCORE, D).astype(np.float32),
        "L0_attn_input_weights": (rng.randn(D, D) * 0.03).astype(np.float32),
        "L0_attn_xis": (rng.rand(D) * 0.5).astype(np.float32),
        "L0_attn_proj_matrix": rng.randn(D, D).astype(np.float32),
        "L0_ffn_w_gate": (rng.randn(FI, D) * 0.02).astype(np.float32),
        "L0_ffn_w_up": (rng.randn(FI, D) * 0.02).astype(np.float32),
        "L0_ffn_w_down": (rng.randn(D, FI) * 0.02).astype(np.float32),
    }
    out = kernel(**ins)
    print("ok", out.shape, out.dtype, float(np.abs(out).mean()))


# version 66
# speedup vs baseline: 1.1721x; 1.1721x over previous
import math
import numpy as np
from contextlib import ExitStack

import concourse.bass as bass
import concourse.mybir as mybir
from concourse.bass_utils import run_bass_kernel_spmd

F32 = mybir.dt.float32
F32R = mybir.dt.float32r
BF16 = mybir.dt.bfloat16
F16 = mybir.dt.float16
I16 = mybir.dt.int16
AF = mybir.ActivationFunctionType
ALU = mybir.AluOpType
AXX = mybir.AxisListType.X

P = 128
D = 1024
FI = 2816
TCORE = 4096
NT = TCORE // P
KD = D // P
KI = FI // P
LN32 = math.log(32.0)
NCHUNKS = [(c * 512, min(512, FI - c * 512)) for c in range((FI + 511) // 512)]

SAFE_SYNC = False

RO_NAB2 = 0
RO_B = D
RO_B2 = 2 * D
RO_ONES = 3 * D
RO_SQ = 3 * D + P
ROWSF_W = 3 * D + 2 * P
CO_AB = 0
CO_AB2 = KD
CO_NAB2 = 2 * KD
CO_XLN = 3 * KD
CO_X512 = 4 * KD
CO_SQW = 5 * KD
COLSCR_W = 6 * KD


def build_core():
    nc = bass.Bass("TRN2", target_bir_lowering=False)
    xd = nc.dram_tensor("x", [TCORE, D], F32, kind="ExternalInput").ap()
    iwd = nc.dram_tensor("iw", [D, D], F32, kind="ExternalInput").ap()
    xisd = nc.dram_tensor("xis", [1, D], F32, kind="ExternalInput").ap()
    projd = nc.dram_tensor("proj", [D, D], F32, kind="ExternalInput").ap()
    wgs = nc.dram_tensor("wgs", [3 * P, D], F32, kind="ExternalInput").ap()
    wus = nc.dram_tensor("wus", [3 * P, D], F32, kind="ExternalInput").ap()
    wds = nc.dram_tensor("wds", [P, FI], F32, kind="ExternalInput").ap()
    gu_in = nc.dram_tensor("gu_in", [6 * P, D], F16)
    gu_out = nc.dram_tensor("gu_out", [48 * P, D], F16)
    wd_in = nc.dram_tensor("wd_in", [P, FI], F16)
    wd_out = nc.dram_tensor("wd_out", [8 * P, FI], F16)
    yd = nc.dram_tensor("y", [TCORE, D], F32, kind="ExternalOutput").ap()

    es = ExitStack()

    def sb(name, shape, dt_=F32):
        return es.enter_context(nc.sbuf_tensor(name, shape, dt_))

    def psum(name, shape, dt_=F32):
        return es.enter_context(nc.psum_tensor(name, shape, dt_))

    projST = sb("projST", [P, KD * D], F16)
    wT = sb("wT", [P, KD * D], F16)
    wgqT = sb("wgqT", [P, 22 * D], F16)
    wuqT = sb("wuqT", [P, 22 * D], F16)
    wdqT = sb("wdqT", [P, KD * FI], F16)
    xb0 = sb("xb0", [P, D])
    xb1 = sb("xb1", [P, D])
    xb2 = sb("xb2", [P, D])
    xbuf = [xb0, xb1, xb2]
    feats = sb("feats", [P, D], BF16)
    feats16 = feats[:].bitcast(F16)
    xq = sb("xq", [P, D], F16)
    hq = sb("hq", [P, FI], F16)
    qa2 = sb("qa2", [P, D], I16)
    qbi = sb("qbi", [P, D], I16)
    qd2 = sb("qd2", [P, D], I16)
    qc1 = sb("qc1", [P, D], BF16)
    qe1 = sb("qe1", [P, D], BF16)
    qm = sb("qm", [P, 64])
    qsc = sb("qsc", [P, 64])
    qrc = qm
    se = sb("se", [P, D])
    qa2b = qa2[:].bitcast(BF16)
    qa2f = qa2[:].bitcast(F16)
    qbib = qbi[:].bitcast(BF16)
    qd2b = qd2[:].bitcast(BF16)
    rowsF = sb("rowsF", [1, ROWSF_W], F16)
    colscr = sb("colscr", [P, COLSCR_W])
    ident_b = sb("ident_b", [P, P], BF16)
    ident_h = sb("ident_h", [P, P], F16)
    ident_f = sb("ident_f", [P, P])
    smalls = sb("smalls", [P, 18])
    one_col = smalls[:, 0:1]
    eps_col = smalls[:, 1:2]
    acc_a = smalls[:, 2:3]
    acc_c = smalls[:, 3:4]
    sqp_c = smalls[:, 4:5]
    LN_SLOTS = {
        1: tuple(smalls[:, 6 + i:7 + i] for i in range(6)),
        2: tuple(smalls[:, 12 + i:13 + i] for i in range(6)),
    }
    ps_dot = psum("ps_dot", [P, 512])
    ps_att = psum("ps_att", [P, 512])
    ps_tp = psum("ps_tp", [P, D], F16)
    ps_g = [psum("ps_g0", [P, 512]), psum("ps_g1", [P, 512])]
    ps_u = [psum("ps_u0", [P, 512]), psum("ps_u1", [P, 512])]
    ps_dn = psum("ps_dn", [P, 512])

    sems = {n: es.enter_context(nc.semaphore("s_" + n))
            for n in ["ld", "l1", "lw", "lx", "lg", "st", "s1", "s2",
                      "pe", "dv", "ac", "gp", "cc"]}
    cnt = {n: 0 for n in sems}
    ops = []
    hw = {"sync": {}, "vector": {}, "scalar": {}, "tensor": {}, "gpsimd": {}}
    OWN = {"vector": "dv", "scalar": "ac", "tensor": "pe", "gpsimd": "gp"}

    def _filter(engine, waits):
        out = []
        h = hw[engine]
        own = OWN.get(engine)
        for sn, val in waits:
            if val <= 0:
                continue
            if sn == own and cnt[sn] - 1 - val >= 2:
                continue
            if h.get(sn, 0) >= val:
                continue
            h[sn] = val
            out.append((sn, val))
        return tuple(out)

    def _add(engine, fn, waits, inc_sem, ret_last=False):
        waits = _filter(engine, waits)

        def f(e, w=waits):
            for sn, val in w:
                e.wait_ge(sems[sn], val)
            last = fn(e)
            if isinstance(last, list):
                last = last[-1] if last else None
            if inc_sem == "pe":
                last.then_inc(sems["pe"], 1)
            elif inc_sem is not None:
                if inc_sem != "gp" and last is not None and hasattr(last, "then_inc"):
                    last.then_inc(sems[inc_sem], 1)
                else:
                    e.drain().then_inc(sems[inc_sem], 1)
        ops.append((engine, f))

    def dma(dst, src, waits=(), ch="ld"):
        cnt[ch] += 16
        v = cnt[ch]
        if SAFE_SYNC:
            waits = tuple(waits) + ((ch, v - 16),)
        waits = _filter("sync", waits)

        def f(e, w=waits, d=dst, s=src, c=ch):
            for sn, val in w:
                e.wait_ge(sems[sn], val)
            e.dma_start(out=d, in_=s).then_inc(sems[c], 16)
        ops.append(("sync", f))
        return v

    def dma_out(dst, src, waits=(), ch="st"):
        cnt[ch] += 16
        v = cnt[ch]
        if SAFE_SYNC:
            waits = tuple(waits) + ((ch, v - 16),)
        waits = _filter("sync", waits)

        def f(e, w=waits, d=dst, s=src, c=ch):
            for sn, val in w:
                e.wait_ge(sems[sn], val)
            e.dma_start(out=d, in_=s).then_inc(sems[c], 16)
        ops.append(("sync", f))
        return v

    def dve(fn, waits=()):
        cnt["dv"] += 1
        _add("vector", fn, waits, "dv")
        return cnt["dv"]

    def act(fn, waits=()):
        cnt["ac"] += 1
        _add("scalar", fn, waits, "ac")
        return cnt["ac"]

    def pe(fn, waits=()):
        cnt["pe"] += 1
        _add("tensor", fn, waits, "pe", ret_last=True)
        return cnt["pe"]

    def gp(fn, waits=()):
        cnt["gp"] += 1
        _add("gpsimd", fn, waits, "gp")
        return cnt["gp"]

    last_tp_evac = [("ac", 0)]

    ps_tp_bf = ps_tp[:].bitcast(BF16)

    def transpose_group(src, src_offs, dst_view, pe_waits=(), bf=False):
        n = len(src_offs)
        ident = ident_b if bf else ident_h
        tp = ps_tp_bf if bf else ps_tp[:]

        def fpe(e):
            last = None
            for j, so in enumerate(src_offs):
                last = e.transpose(tp[:, j * P:(j + 1) * P],
                                   src[:, so:so + P], ident[:])
            return last
        vpe = pe(fpe, waits=tuple(pe_waits) + (last_tp_evac[0],))
        ve = act(lambda e: e.copy(dst_view, tp[:, :n * P]),
                 waits=(("pe", vpe),))
        last_tp_evac[0] = ("ac", ve)
        return ("ac", ve)

    def emit_quant_head(e, src_ap, w, qvp, nb_base=0, qsc_inv=1.0 / 6.0):
        t1 = (w // 3) // 16 * 16
        t2 = ((w - t1) // 2) // 16 * 16
        parts = [(0, t1), (t1, t2), (t1 + t2, w - t1 - t2)]
        pn = [(o // 16, cw // 16) for o, cw in parts]

        def allp(f):
            last = None
            for (o, cw), (nb0, nb) in zip(parts, pn):
                last = f(o, cw, nb0, nb)
            if SAFE_SYNC:
                last = e.drain()
            return last

        def blk(ap, o, cw):
            return ap[:, o:o + cw].rearrange("p (b k) -> p b k", k=16)

        nbb = nb_base
        allp(lambda o, cw, nb0, nb: e.tensor_reduce(
            qm[:, nbb + nb0:nbb + nb0 + nb], blk(src_ap, o, cw), AXX, ALU.max,
            apply_absolute_value=True))
        allp(lambda o, cw, nb0, nb: e.tensor_scalar(
            qsc[:, nbb + nb0:nbb + nb0 + nb], qm[:, nbb + nb0:nbb + nb0 + nb],
            qsc_inv, 1e-12, ALU.mult, ALU.max))
        allp(lambda o, cw, nb0, nb: e.reciprocal(
            qrc[:, nbb + nb0:nbb + nb0 + nb], qm[:, nbb + nb0:nbb + nb0 + nb]))
        allp(lambda o, cw, nb0, nb: e.tensor_scalar(
            qrc[:, nbb + nb0:nbb + nb0 + nb], qrc[:, nbb + nb0:nbb + nb0 + nb],
            6.0, 1e12, ALU.mult, ALU.min))
        return allp(lambda o, cw, nb0, nb: e.tensor_tensor(
            blk(qvp, o, cw), blk(src_ap, o, cw),
            qrc[:, nbb + nb0:nbb + nb0 + nb].unsqueeze(-1)
            .broadcast_to([P, nb, 16]), ALU.mult))

    def emit_quant_conv(e, w, qvp):
        e.activation(qa2[:, 0:w], qvp[:, 0:w], AF.Identity, scale=2.0)
        e.activation(qbi[:, 0:w], qvp[:, 0:w], AF.Identity)
        return e.activation(qd2[:, 0:w], qvp[:, 0:w], AF.Identity, scale=0.5)

    def emit_quant_tail(e, dst_ap, w, dve_conv=False, vbuf=None):
        nb = w // 16

        def blk(ap):
            return ap[:, 0:w].rearrange("p (b k) -> p b k", k=16)

        def s(x):
            if SAFE_SYNC:
                e.drain()
            return x
        if dve_conv:
            s(e.tensor_scalar(qa2[:, 0:w], vbuf[:, 0:w], 2.0, None, ALU.mult))
            s(e.tensor_copy(qbi[:, 0:w], vbuf[:, 0:w]))
            s(e.tensor_scalar(qd2[:, 0:w], vbuf[:, 0:w], 0.5, None, ALU.mult))
        s(e.tensor_scalar(qa2b[:, 0:w], qa2[:, 0:w], 4.0, -4.0, ALU.min, ALU.max))
        s(e.tensor_scalar(qc1[:, 0:w], qbi[:, 0:w], 4.0, -4.0, ALU.min, ALU.max))
        s(e.tensor_scalar(qbib[:, 0:w], qbi[:, 0:w], 2.0, -2.0, ALU.min, ALU.max))
        s(e.tensor_scalar(qe1[:, 0:w], qd2[:, 0:w], 3.0, -3.0, ALU.min, ALU.max))
        s(e.tensor_scalar(qd2b[:, 0:w], qd2[:, 0:w], 2.0, -2.0, ALU.min, ALU.max))
        s(e.tensor_tensor(qc1[:, 0:w], qc1[:, 0:w], qbib[:, 0:w], ALU.subtract))
        s(e.tensor_tensor(qe1[:, 0:w], qe1[:, 0:w], qd2b[:, 0:w], ALU.subtract))
        s(e.tensor_scalar(qa2b[:, 0:w], qa2b[:, 0:w], 0.5, None, ALU.mult))
        s(e.tensor_scalar(qe1[:, 0:w], qe1[:, 0:w], 2.0, None, ALU.mult))
        s(e.tensor_tensor(qc1[:, 0:w], qc1[:, 0:w], qa2b[:, 0:w], ALU.add))
        s(e.tensor_tensor(qc1[:, 0:w], qc1[:, 0:w], qe1[:, 0:w], ALU.add))
        return s(e.tensor_tensor(
            blk(dst_ap), blk(qc1),
            qsc[:, 0:nb].unsqueeze(-1).broadcast_to([P, nb, 16]), ALU.mult))

    def emit_quant_body(e, src_ap, dst_ap, w):
        emit_quant_head(e, src_ap, w, src_ap)
        return emit_quant_tail(e, dst_ap, w, dve_conv=True, vbuf=src_ap)

    def col_to_row(col_ap, dst_row_off, width, waits=()):
        v = 0
        for c0 in range(0, width // P, 4):
            cn = min(4, width // P - c0)

            def fpe(e, c0=c0, cn=cn):
                last = None
                for j in range(cn):
                    last = e.transpose(ps_dot[0:1, j * P:(j + 1) * P],
                                       col_ap[:, c0 + j:c0 + j + 1], ident_f[:])
                return last
            vpe = pe(fpe, waits=tuple(waits) + ((("dv", v),) if v else ()))
            v = dve(lambda e, c0=c0, cn=cn: e.tensor_copy(
                rowsF[0:1, dst_row_off + c0 * P:dst_row_off + (c0 + cn) * P],
                ps_dot[0:1, 0:cn * P]),
                waits=(("pe", vpe),))
        return v

    v_gp0 = gp(lambda e: [
        e.memset(ident_b[:], 0.0),
        e.memset(ident_f[:], 0.0),
        e.memset(ident_h[:], 0.0),
        e.drain(),
        e.affine_select(out=ident_b[:], in_=ident_b[:], compare_op=ALU.not_equal,
                        fill=1.0, base=0, pattern=[[-1, P]], channel_multiplier=1),
        e.affine_select(out=ident_f[:], in_=ident_f[:], compare_op=ALU.not_equal,
                        fill=1.0, base=0, pattern=[[-1, P]], channel_multiplier=1),
        e.affine_select(out=ident_h[:], in_=ident_h[:], compare_op=ALU.not_equal,
                        fill=1.0, base=0, pattern=[[-1, P]], channel_multiplier=1),
        e.memset(one_col, 1.0),
        e.memset(eps_col, 1e-5),
        e.memset(rowsF[0:1, RO_ONES:RO_ONES + P], 1.0),
    ])
    v_ones = dve(lambda e: e.memset(smalls[:, 5:6], 0.0),
                 waits=(("gp", v_gp0),))

    v_xis = dma(xb1[0:1, 0:D], xisd[:])

    def f_abcol(e):
        last = None
        for j in range(KD):
            last = e.transpose(ps_g[0][:, j:j + 1], xb1[0:1, j * P:(j + 1) * P],
                               ident_f[0:1, 0:1])
        return last
    v_abpe = pe(f_abcol, waits=(("ld", v_xis), ("gp", v_gp0)))
    v_abcol = dve(lambda e: e.tensor_copy(colscr[:, CO_AB:CO_AB + KD],
                                          ps_g[0][:, 0:KD]),
                  waits=(("pe", v_abpe),))

    v_cols = dve(lambda e: [
        e.tensor_tensor(colscr[:, CO_AB2:CO_AB2 + KD], colscr[:, CO_AB:CO_AB + KD],
                        colscr[:, CO_AB:CO_AB + KD], ALU.mult),
        e.drain(),
        e.tensor_scalar(colscr[:, CO_NAB2:CO_NAB2 + KD],
                        colscr[:, CO_AB2:CO_AB2 + KD], -1.0, None, ALU.mult),
        e.tensor_scalar(colscr[:, CO_XLN:CO_XLN + KD],
                        colscr[:, CO_AB:CO_AB + KD], -LN32, None, ALU.add),
        e.drain(),
        e.scalar_tensor_tensor(colscr[:, CO_X512:CO_X512 + KD],
                               colscr[:, CO_AB2:CO_AB2 + KD], -512.0,
                               colscr[:, CO_XLN:CO_XLN + KD], ALU.mult, ALU.add),
    ], waits=(("dv", v_abcol),))

    v_nab2row = col_to_row(colscr[:, CO_NAB2:CO_NAB2 + KD], RO_NAB2, D,
                           waits=(("dv", v_cols),))

    projST_3d = projST[:].rearrange("p (k q) -> p k q", q=D)
    iwpark_3d = wgqT[:].rearrange("p (j q) -> p j q", q=D)
    wgqT_j3 = wgqT[:].rearrange("p (j q) -> p j q", q=D)
    wuqT_j3 = wuqT[:].rearrange("p (j q) -> p j q", q=D)
    wdqT_j3 = wdqT[:].rearrange("p (j q) -> p j q", q=FI)

    se16 = se[:].bitcast(F16)
    stbuf = [xb0, xb1]
    prev_use = {0: ("dv", v_ones), 1: ("pe", v_abpe)}
    item_idx = [0]
    prev_contrib = [0]

    def stage_load(src_ap, pw=D):
        p = item_idx[0] % 2
        item_idx[0] += 1
        buf = stbuf[p]
        ch = "ld" if p == 0 else "l1"
        v_l = dma(buf[:, :pw], src_ap, waits=(prev_use[p],), ch=ch)
        return buf, (ch, v_l), p

    def shard_item(srcrows, contrib_rows):
        buf, w_l, p = stage_load(srcrows)
        vq = dve(lambda e, b=buf: emit_quant_body(e, b[:], xq[:], D),
                 waits=(w_l, last_tp_evac[0]))
        prev_use[p] = ("dv", vq)
        wev = transpose_group(xq, [k * P for k in range(KD)], se16[:, 0:D],
                              pe_waits=(("dv", vq), ("st", prev_contrib[0])))
        vc = dma_out(contrib_rows, se16[:, 0:D], waits=(wev,))
        prev_contrib[0] = vc
        return vc

    for i in range(3):
        v_contrib_gu = shard_item(wus[i * P:(i + 1) * P, :],
                                  gu_in[i * P:(i + 1) * P, :])
    for i in range(3):
        v_contrib_gu = shard_item(wgs[i * P:(i + 1) * P, :],
                                  gu_in[(3 + i) * P:(4 + i) * P, :])
    for pc in range(3):
        p0 = pc * D
        pw = min(D, FI - p0)
        buf, w_l, p = stage_load(wds[:, p0:p0 + pw], pw)
        vq = dve(lambda e, b=buf, c=pw: emit_quant_body(e, b[:, 0:c],
                                                        xq[:, 0:c], c),
                 waits=(w_l, last_tp_evac[0]))
        prev_use[p] = ("dv", vq)
        wev = transpose_group(xq, [c * P for c in range(pw // P)],
                              hq[:, p0:p0 + pw], pe_waits=(("dv", vq),))
    v_contrib_wd = dma_out(wd_in.ap(), hq[:, 0:FI], waits=(wev,))

    v_cc = gp(lambda e: [
        e.wait_ge(sems["st"], v_contrib_gu),
        e.collective_compute("AllGather", ALU.bypass,
                             replica_groups=[list(range(8))],
                             ins=[gu_in.ap().opt()], outs=[gu_out.ap().opt()]
                             ).then_inc(sems["cc"], 1),
        e.wait_ge(sems["st"], v_contrib_wd),
        e.collective_compute("AllGather", ALU.bypass,
                             replica_groups=[list(range(8))],
                             ins=[wd_in.ap().opt()], outs=[wd_out.ap().opt()]
                             ).then_inc(sems["cc"], 1),
        e.wait_ge(sems["cc"], 2),
    ] + [e.dma_start(out=wdqT[:, j * FI:(j + 1) * FI],
                     in_=wd_out.ap()[j * P:(j + 1) * P, :]
                     ).then_inc(sems["lg"], 16) for j in range(KD)]
      + [e.wait_ge(sems["ac"], v_wT_exp)]
      + [e.dma_start(out=wgqT[:, j * D:(j + 1) * D],
                     in_=gu_out.ap()[((j // 3) * 6 + 3 + (j % 3)) * P:
                                     ((j // 3) * 6 + 4 + (j % 3)) * P, :]
                     ).then_inc(sems["lg"], 16) for j in range(22)])

    for j in range(KD):
        buf, w_l, p = stage_load(projd[j * P:(j + 1) * P, :])
        v_s = dve(lambda e, b=buf, jj=j: e.tensor_scalar(
            xq[:], b[:], colscr[:, CO_AB + jj:CO_AB + jj + 1], None, ALU.mult),
            waits=(w_l, ("dv", v_cols), last_tp_evac[0]))
        prev_use[p] = ("dv", v_s)
        transpose_group(xq, [k * P for k in range(KD)],
                        projST_3d[:, :, j * P:(j + 1) * P], pe_waits=(("dv", v_s),))

    prev_sq = 0
    for j in range(KD):
        buf, w_l, p = stage_load(iwd[j * P:(j + 1) * P, :])
        v_sq = act(lambda e, b=buf: [e.activation(se[:], b[:], AF.Square,
                                                  accum_out=acc_a), e.drain()],
                   waits=(w_l, ("dv", prev_sq),
                          ("st", v_contrib_gu)))
        prev_sq = dve(lambda e, jj=j: e.tensor_scalar(
            colscr[:, CO_SQW + jj:CO_SQW + jj + 1], acc_a, 0.5, None, ALU.mult),
            waits=(("ac", v_sq),))
        v_c = dve(lambda e, b=buf: e.tensor_copy(xq[:], b[:]),
                  waits=(("ac", v_sq), last_tp_evac[0]))
        prev_use[p] = ("dv", v_c)
        transpose_group(xq, [k * P for k in range(KD)],
                        iwpark_3d[:, 0:KD, j * P:(j + 1) * P], pe_waits=(("dv", v_c),))
    prev_consume = max(prev_use[0], prev_use[1], key=lambda w: w[1])

    v_sqwrow = col_to_row(colscr[:, CO_SQW:CO_SQW + KD], RO_B, D,
                          waits=(("dv", prev_sq),))

    v_wT_exp = 0
    for i in range(KD):
        for oc in range(2):
            def f_mm(e, ii=i, occ=oc):
                for k in range(KD):
                    e.matmul(ps_dot[:, 0:512],
                             projST_3d[:, k, ii * P:(ii + 1) * P],
                             iwpark_3d[:, k, occ * 512:(occ + 1) * 512],
                             start=(k == 0), stop=False)
                return e.matmul(ps_dot[:, 0:512],
                                rowsF[0:1, RO_NAB2 + ii * P:RO_NAB2 + (ii + 1) * P],
                                rowsF[0:1, RO_B + occ * 512:RO_B + (occ + 1) * 512],
                                start=False, stop=True)
            v_mm = pe(f_mm, waits=(("dv", max(v_sqwrow, v_nab2row)),
                                   prev_consume, last_tp_evac[0],
                                   ("ac", v_wT_exp)))
            v_wT_exp = act(lambda e, ii=i, occ=oc: e.activation(
                wT[:, ii * D + occ * 512: ii * D + (occ + 1) * 512],
                ps_dot[:, 0:512], AF.Exp,
                bias=colscr[:, CO_XLN + ii:CO_XLN + ii + 1]),
                waits=(("pe", v_mm),))

    v_x512row = 0
    for c0 in (0, 4):
        def f_x512pe(e, c0=c0):
            last = None
            for j in range(4):
                last = e.transpose(ps_dot[0:1, j * P:(j + 1) * P],
                                   colscr[:, CO_X512 + c0 + j:CO_X512 + c0 + j + 1],
                                   ident_f[:])
            return last
        v_x512pe = pe(f_x512pe, waits=(("ac", v_wT_exp), ("dv", v_sqwrow),
                                       ("dv", v_x512row)))
        v_x512row = dve(lambda e, c0=c0: [
            e.tensor_copy(rowsF[0:1, RO_B + c0 * P:RO_B + (c0 + 4) * P],
                          ps_dot[0:1, 0:512]),
            e.drain(),
            e.tensor_tensor(rowsF[0:1, RO_B2 + c0 * P:RO_B2 + (c0 + 4) * P],
                            ps_dot[0:1, 0:512],
                            rowsF[0:1, RO_B + c0 * P:RO_B + (c0 + 4) * P],
                            ALU.subtract),
        ], waits=(("pe", v_x512pe),))

    v_wload = 0
    for j in range(22):
        r, i = divmod(j, 3)
        v_wload = dma(wuqT[:, j * D:(j + 1) * D],
                      gu_out.ap()[(r * 6 + i) * P:(r * 6 + i + 1) * P, :],
                      waits=(("cc", 1),), ch="lw")


    st = [dict() for _ in range(NT + 1)]
    st[0]["att1"] = 0
    LCH = ["ld", "l1", "lx"]
    SCH = ["st", "s1", "s2"]
    vx = [0] * (NT + 3)
    vx[0] = dma(xb0[:], xd[0:P, :], waits=(("dv", v_x512row),), ch=LCH[0])
    vx[1] = dma(xb1[:], xd[P:2 * P, :], waits=(("dv", v_x512row),), ch=LCH[1])
    vx[2] = dma(xb2[:], xd[2 * P:3 * P, :], waits=(), ch=LCH[2])

    def g(t, key, default=0):
        return st[t][key] if t >= 0 and key in st[t] else default

    def emit_A_head(t):
        xb = xbuf[t % 3]
        s = st[t]
        s["cast"] = dve(lambda e, b=xb: e.tensor_copy(feats16[:], b[:]),
                        waits=((LCH[t % 3], vx[t]), ("pe", g(t - 1, "att1"))))
        v_sqp = dve(lambda e, b=xb: [
            e.scalar_tensor_tensor(hq[:, 0:D], b[:], 1.0, b[:], ALU.mult,
                                   ALU.mult, accum_out=acc_a),
            e.drain(),
            e.tensor_scalar(sqp_c, acc_a, 0.5, -512.0, ALU.mult, ALU.add),
        ], waits=((LCH[t % 3], vx[t]),))
        s["sqp"] = v_sqp
        v_sqT = pe(lambda e: e.transpose(ps_att[0:1, 0:P], sqp_c, ident_f[:]),
                   waits=(("dv", max(v_sqp, g(t - 1, "res"))),))
        s["corr"] = dve(lambda e: e.tensor_copy(rowsF[0:1, RO_SQ:RO_SQ + P],
                                                ps_att[0:1, 0:P]),
                        waits=(("pe", v_sqT), ("dv", g(t - 1, "corr"))))
        s["xT"] = transpose_group(feats16, [k * P for k in range(KD)],
                                  feats16[:, 0:D], pe_waits=(("dv", s["cast"]),))

    def emit_dot_h(t, b):
        s = st[t]

        def f_dot(e, bb=b, ps=(ps_dot if b == 0 else ps_att)):
            for k in range(KD):
                e.matmul(ps[:, 0:512],
                         feats16[:, k * P:(k + 1) * P],
                         projST_3d[:, k, bb * 512:(bb + 1) * 512],
                         start=(k == 0), stop=False)
            e.matmul(ps[:, 0:512],
                     rowsF[0:1, RO_SQ:RO_SQ + P],
                     rowsF[0:1, RO_NAB2 + bb * 512:RO_NAB2 + (bb + 1) * 512],
                     start=False, stop=False)
            e.matmul(ps[:, 0:512],
                     rowsF[0:1, RO_ONES:RO_ONES + P],
                     rowsF[0:1, RO_B + bb * 512:RO_B + (bb + 1) * 512],
                     start=False, stop=False)
            return e.matmul(ps[:, 0:512],
                            rowsF[0:1, RO_ONES:RO_ONES + P],
                            rowsF[0:1, RO_B2 + bb * 512:RO_B2 + (bb + 1) * 512],
                            start=False, stop=True)
        w = [s["xT"], ("dv", s["corr"])]
        if t == 0:
            w.append(("dv", v_x512row))
        if b == 0:
            w.append(("ac", g(t - 1, "exp", v_wT_exp)))
        else:
            w.append(("dv", g(t - 1, "res")))
        s[f"dot{b}"] = pe(f_dot, waits=tuple(w))
        if b == 1:
            s["exp"] = act(lambda e: [
                e.activation(feats[:, 0:512], ps_dot[:, 0:512], AF.Exp),
                e.activation(feats[:, 512:D], ps_att[:, 0:512], AF.Exp),
            ], waits=(("pe", s["dot1"]),))
            s["fT"] = transpose_group(feats, [i * P for i in range(KD)],
                                      feats[:, 0:D],
                                      pe_waits=(("ac", s["exp"]),), bf=True)

    def emit_att_h(t, b):
        s = st[t]

        def f_att(e, bb=b):
            last = None
            for i in range(KD):
                last = e.matmul(ps_att[:, 0:512],
                                feats[:, i * P:(i + 1) * P],
                                wT[:, i * D + bb * 512: i * D + (bb + 1) * 512],
                                start=(i == 0), stop=(i == KD - 1))
            return last
        xb = xbuf[t % 3]
        if b == 0:
            v = pe(f_att, waits=(s["fT"], ("ac", s["exp"])))
            s["add0"] = dve(lambda e, bx=xb: e.tensor_tensor(
                bx[:, 0:512], bx[:, 0:512], ps_att[:, 0:512], ALU.add),
                waits=(("pe", v),))
        else:
            v = pe(f_att, waits=(s["fT"], ("dv", s["add0"])))
            s["att1"] = v
            s["res"] = dve(lambda e, bx=xb: [
                e.tensor_tensor(bx[:, 512:D], bx[:, 512:D], ps_att[:, 0:512],
                                ALU.add),
                e.tensor_scalar(qa2f[:, 0:D], bx[:], 1.0, None, ALU.mult,
                                ALU.add, accum_out=LN_SLOTS[1][1]),
            ], waits=(("pe", v),))

    def emit_ln(which, v_ready, xbt):
        s_sq, s_sum, s_mean, s_var, s_rstd, s_nmb = LN_SLOTS[which]
        junk = se[:] if which == 1 else hq[:, 0:D]
        v_s2 = act(lambda e, j=junk: [e.activation(j, xbt[:], AF.Square,
                                                   accum_out=s_sq), e.drain()],
                   waits=(("dv", v_ready),))
        if which == 2:
            cur_ln2sq[0] = v_s2
        v_var = dve(lambda e: [
            e.scalar_tensor_tensor(s_var, s_sum, 1.0 / D, s_sum, ALU.mult, ALU.mult),
            e.tensor_scalar(s_mean, s_sum, -1.0 / D, None, ALU.mult),
            e.drain(),
            e.tensor_tensor(s_var, s_sq, s_var, ALU.subtract),
        ], waits=(("ac", v_s2),))
        return act(lambda e: [
            e.activation(s_rstd, s_var, AF.Ln, bias=eps_col, scale=1.0 / D),
            e.drain(),
            e.activation(s_rstd, s_rstd, AF.Exp, scale=-0.5),
            e.drain(),
            e.activation(s_nmb, s_mean, AF.Identity, scale=s_rstd),
            e.drain(),
            e.activation(xbt[:], xbt[:], AF.Identity, bias=s_nmb, scale=s_rstd),
        ], waits=(("dv", v_var),))

    cur_ln2sq = [0]

    def emit_ln1(t):
        s = st[t]
        s["x1"] = emit_ln(1, max(s["res"], g(t - 1, "hq2")), xbuf[t % 3])

    def emit_xq_head(t):
        xb = xbuf[t % 3]
        s = st[t]
        s["xqh"] = dve(lambda e, b=xb: emit_quant_head(e, b[:], D, se),
                       waits=(("ac", s["x1"]),))

    def emit_xq_tail(t):
        s = st[t]
        s["xq"] = dve(lambda e: emit_quant_tail(e, xq[:], D, dve_conv=True,
                                                vbuf=se),
                      waits=(("pe", g(t - 1, "mmv5")),))

    def emit_xqT(t):
        s = st[t]
        s["xqT"] = transpose_group(xq, [k * P for k in range(KD)], xq[:, 0:D],
                                   pe_waits=(("dv", s["xq"]),))

    CHUNKS = [(h * 512, min(512, FI - h * 512)) for h in range(6)]

    def emit_chunk(t, h):
        s = st[t]
        off, cw = CHUNKS[h]
        p = h % 2
        seo = p * 512
        nj = cw // P
        j0 = h * 4

        def f_mm_g(e, cw=cw, p=p, j0=j0, nj=nj):
            last = None
            for k in range(KD):
                last = e.matmul(ps_g[p][:, 0:cw], xq[:, k * P:(k + 1) * P],
                                wgqT_j3[:, j0:j0 + nj, k * P:(k + 1) * P],
                                start=(k == 0), stop=(k == KD - 1))
            return last

        def f_mm_u(e, cw=cw, p=p, j0=j0, nj=nj):
            last = None
            for k in range(KD):
                last = e.matmul(ps_u[p][:, 0:cw], xq[:, k * P:(k + 1) * P],
                                wuqT_j3[:, j0:j0 + nj, k * P:(k + 1) * P],
                                start=(k == 0), stop=(k == KD - 1))
            return last
        w = []
        wu = []
        if h == 0:
            w.append(s["xqT"])
        if h < 2:
            w.append(("dv", g(t - 1, "mulg%d" % (4 + h))))
            wu.append(("dv", g(t - 1, "mul%d" % (4 + h))))
            if t == 0:
                w.append(("lw", v_wload))
                w.append(("lg", 16 * (KD + 22)))
        else:
            w.append(("dv", s["mulg%d" % (h - 2)]))
            wu.append(("dv", s["mul%d" % (h - 2)]))
        v_mm = pe(f_mm_g, waits=tuple(w))
        v_mmu = pe(f_mm_u, waits=tuple(wu))
        s["mmv%d" % h] = v_mmu

        pair_prev = h // 2 - 1
        v_sig = act(lambda e, seo=seo, cw=cw, p=p: e.activation(
            se[:, seo:seo + cw], ps_g[p][:, 0:cw], AF.Tanh, scale=0.5),
            waits=(("pe", v_mm),
                   ("dv", s["hq%d" % pair_prev] if pair_prev >= 0
                    else s["xq"])))

        s["mulg%d" % h] = dve(lambda e, seo=seo, cw=cw, p=p:
                              e.scalar_tensor_tensor(
                                  se[:, seo:seo + cw], se[:, seo:seo + cw],
                                  1.0, ps_g[p][:, 0:cw], ALU.add, ALU.mult),
                              waits=(("ac", v_sig),))
        s["mul%d" % h] = dve(lambda e, seo=seo, cw=cw, p=p: e.tensor_tensor(
            se[:, seo:seo + cw], se[:, seo:seo + cw], ps_u[p][:, 0:cw],
            ALU.mult), waits=(("pe", v_mmu),))

    def emit_pair_qhA(t, pi):
        s = st[t]
        s["qa%d" % pi] = dve(lambda e: emit_quant_head(
            e, se[:, 0:512], 512, se, qsc_inv=1.0 / 12.0),
            waits=(("dv", s["mul%d" % (2 * pi)]),))

    def emit_pair_quant(t, pi):
        s = st[t]
        pw = 1024 if pi < 2 else 768
        s["qh%d" % pi] = dve(lambda e, c=pw - 512: emit_quant_head(
            e, se[:, 512:512 + c], c, se[:, 512:], nb_base=32,
            qsc_inv=1.0 / 12.0),
            waits=(("dv", s["mul%d" % (2 * pi + 1)]),))
        s["hq%d" % pi] = dve(lambda e, o=1024 * pi, c=pw: emit_quant_tail(
            e, hq[:, o:o + c], c, dve_conv=True, vbuf=se),
            waits=(("pe", g(t - 1, "dno1") if pi == 0 else 0),
                   ("ac", g(t - 1, "ln2sq") if pi == 0 else 0),
                   ("lw", v_wload if (t == 0 and pi == 0) else 0),
                   ("lg", 16 * (KD + 22) if (t == 0 and pi == 0) else 0)))

    DOWN_GROUPS = [list(range(0, 8)), list(range(8, 16)), list(range(16, 22))]

    def emit_Thq_down(t, pgrp):
        s = st[t]
        cs = DOWN_GROUPS[pgrp]
        wev = transpose_group(hq, [c * P for c in cs],
                              hq[:, cs[0] * P:(cs[-1] + 1) * P],
                              pe_waits=(("dv", s["hq%d" % pgrp]),))

        def f_d(e, cs=tuple(cs), first=(pgrp == 0), lastg=(pgrp == 2)):
            last = None
            for i, c in enumerate(cs):
                last = e.matmul(ps_dn[:, 0:512], hq[:, c * P:(c + 1) * P],
                                wdqT_j3[:, 0:4, c * P:(c + 1) * P],
                                start=(first and i == 0),
                                stop=(lastg and i == len(cs) - 1))
            return last
        w = [wev]
        if pgrp == 0:
            w.append(("dv", g(t - 1, "res2")))
        v = pe(f_d, waits=tuple(w))
        if pgrp == 2:
            s["dno0"] = v
            s["ThqL"] = wev

    def emit_down_o1a(t):
        s = st[t]
        xb = xbuf[t % 3]
        s["res2h0"] = dve(lambda e, b=xb: e.tensor_tensor(
            b[:, 0:512], b[:, 0:512], ps_dn[:, 0:512], ALU.add),
            waits=(("pe", s["dno0"]),))

        def f_d1(e):
            last = None
            for c in range(KI):
                last = e.matmul(ps_dn[:, 0:512], hq[:, c * P:(c + 1) * P],
                                wdqT_j3[:, 4:8, c * P:(c + 1) * P],
                                start=(c == 0), stop=(c == KI - 1))
            return last
        s["dno1"] = pe(f_d1, waits=(("dv", s["res2h0"]), s["ThqL"]))

    def emit_down_o1b(t):
        s = st[t]
        xb = xbuf[t % 3]
        s["res2"] = dve(lambda e, b=xb: [
            e.tensor_tensor(b[:, 512:D], b[:, 512:D], ps_dn[:, 0:512], ALU.add),
            e.tensor_scalar(qa2f[:, 0:D], b[:], 1.0, None, ALU.mult, ALU.add,
                            accum_out=LN_SLOTS[2][1]),
        ], waits=(("pe", s["dno1"]),))

    def emit_ln2_store(t):
        xb = xbuf[t % 3]
        s = st[t]
        v_y = emit_ln(2, s["res2"], xb)
        s["ln2sq"] = cur_ln2sq[0]
        vs = dma_out(yd[t * P:(t + 1) * P, :], xb[:], waits=(("ac", v_y),),
                     ch=SCH[t % 3])
        if t + 3 < NT:
            vx[t + 3] = dma(xbuf[t % 3][:], xd[(t + 3) * P:(t + 4) * P, :],
                            waits=((SCH[t % 3], vs),), ch=LCH[t % 3])

    emit_A_head(0)
    emit_dot_h(0, 0)
    emit_dot_h(0, 1)
    emit_att_h(0, 0)
    emit_att_h(0, 1)
    emit_ln1(0)
    emit_xq_head(0)
    emit_xq_tail(0)
    emit_xqT(0)
    for t in range(NT):
        if t + 1 < NT:
            emit_A_head(t + 1)
        emit_chunk(t, 0)
        emit_pair_qhA(t, 0)
        emit_chunk(t, 1)
        if t + 1 < NT:
            emit_dot_h(t + 1, 0)
        emit_pair_quant(t, 0)
        emit_chunk(t, 2)
        emit_pair_qhA(t, 1)
        if t + 1 < NT:
            emit_dot_h(t + 1, 1)
        emit_chunk(t, 3)
        emit_pair_quant(t, 1)
        emit_Thq_down(t, 0)
        if t + 1 < NT:
            emit_att_h(t + 1, 0)
        emit_chunk(t, 4)
        emit_pair_qhA(t, 2)
        emit_chunk(t, 5)
        if t + 1 < NT:
            emit_att_h(t + 1, 1)
        emit_pair_quant(t, 2)
        emit_Thq_down(t, 1)
        if t + 1 < NT:
            emit_ln1(t + 1)
        emit_Thq_down(t, 2)
        if t + 1 < NT:
            emit_xq_head(t + 1)
        emit_down_o1a(t)
        if t + 1 < NT:
            emit_xq_tail(t + 1)
        emit_down_o1b(t)
        emit_ln2_store(t)
        if t + 1 < NT:
            emit_xqT(t + 1)

    tot = {c: cnt[c] for c in SCH}
    ops.append(("sync", lambda e: [e.wait_ge(sems[c], tot[c])
                                   for c in SCH if tot[c]]))

    block = es.enter_context(nc.Block())

    def replay(name):
        def f(e):
            for en, fn in ops:
                if en == name:
                    fn(e)
        return f
    block.sync(replay("sync"))
    block.vector(replay("vector"))
    block.scalar(replay("scalar"))
    block.tensor(replay("tensor"))
    block.gpsimd(replay("gpsimd"))
    es.close()
    return nc


_NC_CACHE = None


def kernel(L0_attn_x, L0_attn_input_weights, L0_attn_xis, L0_attn_proj_matrix,
           L0_ffn_w_gate, L0_ffn_w_up, L0_ffn_w_down):
    global _NC_CACHE
    x = np.ascontiguousarray(np.asarray(L0_attn_x, dtype=np.float32)).reshape(8 * TCORE, D)
    iw = np.ascontiguousarray(np.asarray(L0_attn_input_weights, dtype=np.float32))
    xis = np.ascontiguousarray(np.asarray(L0_attn_xis, dtype=np.float32)).reshape(1, D)
    proj = np.ascontiguousarray(np.asarray(L0_attn_proj_matrix, dtype=np.float32))
    wg = np.zeros((24 * P, D), dtype=np.float32)
    wg[:FI] = np.asarray(L0_ffn_w_gate, dtype=np.float32)
    wu = np.zeros((24 * P, D), dtype=np.float32)
    wu[:FI] = np.asarray(L0_ffn_w_up, dtype=np.float32)
    wd = np.ascontiguousarray(np.asarray(L0_ffn_w_down, dtype=np.float32))

    if _NC_CACHE is None:
        _NC_CACHE = build_core()
    nc = _NC_CACHE
    in_maps = []
    for c in range(8):
        in_maps.append({
            "x": x[c * TCORE:(c + 1) * TCORE],
            "iw": iw, "xis": xis, "proj": proj,
            "wgs": np.ascontiguousarray(wg[c * 3 * P:(c + 1) * 3 * P]),
            "wus": np.ascontiguousarray(wu[c * 3 * P:(c + 1) * 3 * P]),
            "wds": np.ascontiguousarray(wd[c * P:(c + 1) * P]),
        })
    res = run_bass_kernel_spmd(nc, in_maps, core_ids=list(range(8)))
    y = np.concatenate([res.results[c]["y"] for c in range(8)], axis=0)
    return y.reshape(8, TCORE, D).astype(np.float32)


if __name__ == "__main__":
    rng = np.random.RandomState(0)
    ins = {
        "L0_attn_x": rng.randn(8, TCORE, D).astype(np.float32),
        "L0_attn_input_weights": (rng.randn(D, D) * 0.03).astype(np.float32),
        "L0_attn_xis": (rng.rand(D) * 0.5).astype(np.float32),
        "L0_attn_proj_matrix": rng.randn(D, D).astype(np.float32),
        "L0_ffn_w_gate": (rng.randn(FI, D) * 0.02).astype(np.float32),
        "L0_ffn_w_up": (rng.randn(FI, D) * 0.02).astype(np.float32),
        "L0_ffn_w_down": (rng.randn(D, FI) * 0.02).astype(np.float32),
    }
    out = kernel(**ins)
    print("ok", out.shape, out.dtype, float(np.abs(out).mean()))


# version 67 (final)
# speedup vs baseline: 1.1941x; 1.0188x over previous
import math
import numpy as np
from contextlib import ExitStack

import concourse.bass as bass
import concourse.mybir as mybir
from concourse.bass_utils import run_bass_kernel_spmd

F32 = mybir.dt.float32
F32R = mybir.dt.float32r
BF16 = mybir.dt.bfloat16
F16 = mybir.dt.float16
I16 = mybir.dt.int16
AF = mybir.ActivationFunctionType
ALU = mybir.AluOpType
AXX = mybir.AxisListType.X

P = 128
D = 1024
FI = 2816
TCORE = 4096
NT = TCORE // P
KD = D // P
KI = FI // P
LN32 = math.log(32.0)
NCHUNKS = [(c * 512, min(512, FI - c * 512)) for c in range((FI + 511) // 512)]

SAFE_SYNC = False

RO_NAB2 = 0
RO_B = D
RO_B2 = 2 * D
RO_ONES = 3 * D
RO_SQ = 3 * D + P
ROWSF_W = 3 * D + 2 * P
CO_AB = 0
CO_AB2 = KD
CO_NAB2 = 2 * KD
CO_XLN = 3 * KD
CO_X512 = 4 * KD
CO_SQW = 5 * KD
COLSCR_W = 6 * KD


def build_core():
    nc = bass.Bass("TRN2", target_bir_lowering=False)
    xd = nc.dram_tensor("x", [TCORE, D], F32, kind="ExternalInput").ap()
    iwd = nc.dram_tensor("iw", [D, D], F32, kind="ExternalInput").ap()
    xisd = nc.dram_tensor("xis", [1, D], F32, kind="ExternalInput").ap()
    projd = nc.dram_tensor("proj", [D, D], F32, kind="ExternalInput").ap()
    wgs = nc.dram_tensor("wgs", [3 * P, D], F32, kind="ExternalInput").ap()
    wus = nc.dram_tensor("wus", [3 * P, D], F32, kind="ExternalInput").ap()
    wds = nc.dram_tensor("wds", [P, FI], F32, kind="ExternalInput").ap()
    gu_in = nc.dram_tensor("gu_in", [6 * P, D], F16)
    gu_out = nc.dram_tensor("gu_out", [48 * P, D], F16)
    wd_in = nc.dram_tensor("wd_in", [P, FI], F16)
    wd_out = nc.dram_tensor("wd_out", [8 * P, FI], F16)
    yd = nc.dram_tensor("y", [TCORE, D], F32, kind="ExternalOutput").ap()

    es = ExitStack()

    def sb(name, shape, dt_=F32):
        return es.enter_context(nc.sbuf_tensor(name, shape, dt_))

    def psum(name, shape, dt_=F32):
        return es.enter_context(nc.psum_tensor(name, shape, dt_))

    projST = sb("projST", [P, KD * D], F16)
    wT = sb("wT", [P, KD * D], F16)
    wgqT = sb("wgqT", [P, 22 * D], F16)
    wuqT = sb("wuqT", [P, 22 * D], F16)
    wdqT = sb("wdqT", [P, KD * FI], F16)
    xb0 = sb("xb0", [P, D])
    xb1 = sb("xb1", [P, D])
    xb2 = sb("xb2", [P, D])
    xbuf = [xb0, xb1, xb2]
    feats = sb("feats", [P, D], BF16)
    feats16 = feats[:].bitcast(F16)
    xq = sb("xq", [P, D], F16)
    hq = sb("hq", [P, FI], F16)
    qa2 = sb("qa2", [P, D], I16)
    qbi = sb("qbi", [P, D], I16)
    qd2 = sb("qd2", [P, D], I16)
    qc1 = sb("qc1", [P, D], BF16)
    qe1 = sb("qe1", [P, D], BF16)
    qm = sb("qm", [P, 64])
    qsc = sb("qsc", [P, 64])
    qrc = qm
    se = sb("se", [P, D])
    qa2b = qa2[:].bitcast(BF16)
    qa2f = qa2[:].bitcast(F16)
    qbib = qbi[:].bitcast(BF16)
    qd2b = qd2[:].bitcast(BF16)
    rowsF = sb("rowsF", [1, ROWSF_W], F16)
    colscr = sb("colscr", [P, COLSCR_W])
    ident_b = sb("ident_b", [P, P], BF16)
    ident_h = sb("ident_h", [P, P], F16)
    ident_f = sb("ident_f", [P, P])
    smalls = sb("smalls", [P, 18])
    one_col = smalls[:, 0:1]
    eps_col = smalls[:, 1:2]
    acc_a = smalls[:, 2:3]
    acc_c = smalls[:, 3:4]
    sqp_c = smalls[:, 4:5]
    LN_SLOTS = {
        1: tuple(smalls[:, 6 + i:7 + i] for i in range(6)),
        2: tuple(smalls[:, 12 + i:13 + i] for i in range(6)),
    }
    ps_dot = psum("ps_dot", [P, 512])
    ps_att = psum("ps_att", [P, 512])
    ps_tp = psum("ps_tp", [P, D], F16)
    ps_g = [psum("ps_g0", [P, 512]), psum("ps_g1", [P, 512])]
    ps_u = [psum("ps_u0", [P, 512]), psum("ps_u1", [P, 512])]
    ps_dn = psum("ps_dn", [P, 512])

    sems = {n: es.enter_context(nc.semaphore("s_" + n))
            for n in ["ld", "l1", "lw", "lx", "lg", "st", "s1", "s2",
                      "pe", "dv", "ac", "gp", "cc"]}
    cnt = {n: 0 for n in sems}
    ops = []
    hw = {"sync": {}, "vector": {}, "scalar": {}, "tensor": {}, "gpsimd": {}}
    OWN = {"vector": "dv", "scalar": "ac", "tensor": "pe", "gpsimd": "gp"}

    def _filter(engine, waits):
        out = []
        h = hw[engine]
        own = OWN.get(engine)
        for sn, val in waits:
            if val <= 0:
                continue
            if sn == own and cnt[sn] - 1 - val >= 2:
                continue
            if h.get(sn, 0) >= val:
                continue
            h[sn] = val
            out.append((sn, val))
        return tuple(out)

    def _add(engine, fn, waits, inc_sem, ret_last=False):
        waits = _filter(engine, waits)

        def f(e, w=waits):
            for sn, val in w:
                e.wait_ge(sems[sn], val)
            last = fn(e)
            if isinstance(last, list):
                last = last[-1] if last else None
            if inc_sem == "pe":
                last.then_inc(sems["pe"], 1)
            elif inc_sem is not None:
                if inc_sem != "gp" and last is not None and hasattr(last, "then_inc"):
                    last.then_inc(sems[inc_sem], 1)
                else:
                    e.drain().then_inc(sems[inc_sem], 1)
        ops.append((engine, f))

    def dma(dst, src, waits=(), ch="ld"):
        cnt[ch] += 16
        v = cnt[ch]
        if SAFE_SYNC:
            waits = tuple(waits) + ((ch, v - 16),)
        waits = _filter("sync", waits)

        def f(e, w=waits, d=dst, s=src, c=ch):
            for sn, val in w:
                e.wait_ge(sems[sn], val)
            e.dma_start(out=d, in_=s).then_inc(sems[c], 16)
        ops.append(("sync", f))
        return v

    def dma_out(dst, src, waits=(), ch="st"):
        cnt[ch] += 16
        v = cnt[ch]
        if SAFE_SYNC:
            waits = tuple(waits) + ((ch, v - 16),)
        waits = _filter("sync", waits)

        def f(e, w=waits, d=dst, s=src, c=ch):
            for sn, val in w:
                e.wait_ge(sems[sn], val)
            e.dma_start(out=d, in_=s).then_inc(sems[c], 16)
        ops.append(("sync", f))
        return v

    def dve(fn, waits=()):
        cnt["dv"] += 1
        _add("vector", fn, waits, "dv")
        return cnt["dv"]

    def act(fn, waits=()):
        cnt["ac"] += 1
        _add("scalar", fn, waits, "ac")
        return cnt["ac"]

    def pe(fn, waits=()):
        cnt["pe"] += 1
        _add("tensor", fn, waits, "pe", ret_last=True)
        return cnt["pe"]

    def gp(fn, waits=()):
        cnt["gp"] += 1
        _add("gpsimd", fn, waits, "gp")
        return cnt["gp"]

    last_tp_evac = [("ac", 0)]

    ps_tp_bf = ps_tp[:].bitcast(BF16)

    def transpose_group(src, src_offs, dst_view, pe_waits=(), bf=False):
        n = len(src_offs)
        ident = ident_b if bf else ident_h
        tp = ps_tp_bf if bf else ps_tp[:]

        def fpe(e):
            last = None
            for j, so in enumerate(src_offs):
                last = e.transpose(tp[:, j * P:(j + 1) * P],
                                   src[:, so:so + P], ident[:])
            return last
        vpe = pe(fpe, waits=tuple(pe_waits) + (last_tp_evac[0],))
        ve = act(lambda e: e.copy(dst_view, tp[:, :n * P]),
                 waits=(("pe", vpe),))
        last_tp_evac[0] = ("ac", ve)
        return ("ac", ve)

    def emit_quant_head(e, src_ap, w, qvp, nb_base=0, qsc_inv=1.0 / 6.0):
        t1 = (w // 3) // 16 * 16
        t2 = ((w - t1) // 2) // 16 * 16
        parts = [(0, t1), (t1, t2), (t1 + t2, w - t1 - t2)]
        pn = [(o // 16, cw // 16) for o, cw in parts]

        def allp(f):
            last = None
            for (o, cw), (nb0, nb) in zip(parts, pn):
                last = f(o, cw, nb0, nb)
            if SAFE_SYNC:
                last = e.drain()
            return last

        def blk(ap, o, cw):
            return ap[:, o:o + cw].rearrange("p (b k) -> p b k", k=16)

        nbb = nb_base
        allp(lambda o, cw, nb0, nb: e.tensor_reduce(
            qm[:, nbb + nb0:nbb + nb0 + nb], blk(src_ap, o, cw), AXX, ALU.max,
            apply_absolute_value=True))
        allp(lambda o, cw, nb0, nb: e.tensor_scalar(
            qsc[:, nbb + nb0:nbb + nb0 + nb], qm[:, nbb + nb0:nbb + nb0 + nb],
            qsc_inv, 1e-12, ALU.mult, ALU.max))
        allp(lambda o, cw, nb0, nb: e.reciprocal(
            qrc[:, nbb + nb0:nbb + nb0 + nb], qm[:, nbb + nb0:nbb + nb0 + nb]))
        allp(lambda o, cw, nb0, nb: e.tensor_scalar(
            qrc[:, nbb + nb0:nbb + nb0 + nb], qrc[:, nbb + nb0:nbb + nb0 + nb],
            6.0, 1e12, ALU.mult, ALU.min))
        return allp(lambda o, cw, nb0, nb: e.tensor_tensor(
            blk(qvp, o, cw), blk(src_ap, o, cw),
            qrc[:, nbb + nb0:nbb + nb0 + nb].unsqueeze(-1)
            .broadcast_to([P, nb, 16]), ALU.mult))

    def emit_quant_conv(e, w, qvp):
        e.activation(qa2[:, 0:w], qvp[:, 0:w], AF.Identity, scale=2.0)
        e.activation(qbi[:, 0:w], qvp[:, 0:w], AF.Identity)
        return e.activation(qd2[:, 0:w], qvp[:, 0:w], AF.Identity, scale=0.5)

    def emit_quant_tail(e, dst_ap, w, dve_conv=False, vbuf=None):
        nb = w // 16

        def blk(ap):
            return ap[:, 0:w].rearrange("p (b k) -> p b k", k=16)

        def s(x):
            if SAFE_SYNC:
                e.drain()
            return x
        if dve_conv:
            s(e.tensor_scalar(qa2[:, 0:w], vbuf[:, 0:w], 2.0, None, ALU.mult))
            s(e.tensor_copy(qbi[:, 0:w], vbuf[:, 0:w]))
            s(e.tensor_scalar(qd2[:, 0:w], vbuf[:, 0:w], 0.5, None, ALU.mult))
        s(e.tensor_scalar(qa2b[:, 0:w], qa2[:, 0:w], 4.0, -4.0, ALU.min, ALU.max))
        s(e.tensor_scalar(qc1[:, 0:w], qbi[:, 0:w], 4.0, -4.0, ALU.min, ALU.max))
        s(e.tensor_scalar(qbib[:, 0:w], qbi[:, 0:w], 2.0, -2.0, ALU.min, ALU.max))
        s(e.tensor_scalar(qe1[:, 0:w], qd2[:, 0:w], 3.0, -3.0, ALU.min, ALU.max))
        s(e.tensor_scalar(qd2b[:, 0:w], qd2[:, 0:w], 2.0, -2.0, ALU.min, ALU.max))
        s(e.tensor_tensor(qc1[:, 0:w], qc1[:, 0:w], qbib[:, 0:w], ALU.subtract))
        s(e.tensor_tensor(qe1[:, 0:w], qe1[:, 0:w], qd2b[:, 0:w], ALU.subtract))
        s(e.tensor_scalar(qa2b[:, 0:w], qa2b[:, 0:w], 0.5, None, ALU.mult))
        s(e.tensor_scalar(qe1[:, 0:w], qe1[:, 0:w], 2.0, None, ALU.mult))
        s(e.tensor_tensor(qc1[:, 0:w], qc1[:, 0:w], qa2b[:, 0:w], ALU.add))
        s(e.tensor_tensor(qc1[:, 0:w], qc1[:, 0:w], qe1[:, 0:w], ALU.add))
        return s(e.tensor_tensor(
            blk(dst_ap), blk(qc1),
            qsc[:, 0:nb].unsqueeze(-1).broadcast_to([P, nb, 16]), ALU.mult))

    def emit_quant_body(e, src_ap, dst_ap, w):
        emit_quant_head(e, src_ap, w, src_ap)
        return emit_quant_tail(e, dst_ap, w, dve_conv=True, vbuf=src_ap)

    def col_to_row(col_ap, dst_row_off, width, waits=()):
        v = 0
        for c0 in range(0, width // P, 4):
            cn = min(4, width // P - c0)

            def fpe(e, c0=c0, cn=cn):
                last = None
                for j in range(cn):
                    last = e.transpose(ps_dot[0:1, j * P:(j + 1) * P],
                                       col_ap[:, c0 + j:c0 + j + 1], ident_f[:])
                return last
            vpe = pe(fpe, waits=tuple(waits) + ((("dv", v),) if v else ()))
            v = dve(lambda e, c0=c0, cn=cn: e.tensor_copy(
                rowsF[0:1, dst_row_off + c0 * P:dst_row_off + (c0 + cn) * P],
                ps_dot[0:1, 0:cn * P]),
                waits=(("pe", vpe),))
        return v

    v_gp0 = gp(lambda e: [
        e.memset(ident_b[:], 0.0),
        e.memset(ident_f[:], 0.0),
        e.memset(ident_h[:], 0.0),
        e.drain(),
        e.affine_select(out=ident_b[:], in_=ident_b[:], compare_op=ALU.not_equal,
                        fill=1.0, base=0, pattern=[[-1, P]], channel_multiplier=1),
        e.affine_select(out=ident_f[:], in_=ident_f[:], compare_op=ALU.not_equal,
                        fill=1.0, base=0, pattern=[[-1, P]], channel_multiplier=1),
        e.affine_select(out=ident_h[:], in_=ident_h[:], compare_op=ALU.not_equal,
                        fill=1.0, base=0, pattern=[[-1, P]], channel_multiplier=1),
        e.memset(one_col, 1.0),
        e.memset(eps_col, 1e-5),
        e.memset(rowsF[0:1, RO_ONES:RO_ONES + P], 1.0),
    ])
    v_ones = dve(lambda e: e.memset(smalls[:, 5:6], 0.0),
                 waits=(("gp", v_gp0),))

    v_xis = dma(xb1[0:1, 0:D], xisd[:])

    def f_abcol(e):
        last = None
        for j in range(KD):
            last = e.transpose(ps_g[0][:, j:j + 1], xb1[0:1, j * P:(j + 1) * P],
                               ident_f[0:1, 0:1])
        return last
    v_abpe = pe(f_abcol, waits=(("ld", v_xis), ("gp", v_gp0)))
    v_abcol = dve(lambda e: e.tensor_copy(colscr[:, CO_AB:CO_AB + KD],
                                          ps_g[0][:, 0:KD]),
                  waits=(("pe", v_abpe),))

    v_cols = dve(lambda e: [
        e.tensor_tensor(colscr[:, CO_AB2:CO_AB2 + KD], colscr[:, CO_AB:CO_AB + KD],
                        colscr[:, CO_AB:CO_AB + KD], ALU.mult),
        e.drain(),
        e.tensor_scalar(colscr[:, CO_NAB2:CO_NAB2 + KD],
                        colscr[:, CO_AB2:CO_AB2 + KD], -1.0, None, ALU.mult),
        e.tensor_scalar(colscr[:, CO_XLN:CO_XLN + KD],
                        colscr[:, CO_AB:CO_AB + KD], -LN32, None, ALU.add),
        e.drain(),
        e.scalar_tensor_tensor(colscr[:, CO_X512:CO_X512 + KD],
                               colscr[:, CO_AB2:CO_AB2 + KD], -512.0,
                               colscr[:, CO_XLN:CO_XLN + KD], ALU.mult, ALU.add),
    ], waits=(("dv", v_abcol),))

    v_nab2row = col_to_row(colscr[:, CO_NAB2:CO_NAB2 + KD], RO_NAB2, D,
                           waits=(("dv", v_cols),))

    projST_3d = projST[:].rearrange("p (k q) -> p k q", q=D)
    iwpark_3d = wgqT[:].rearrange("p (j q) -> p j q", q=D)
    wgqT_j3 = wgqT[:].rearrange("p (j q) -> p j q", q=D)
    wuqT_j3 = wuqT[:].rearrange("p (j q) -> p j q", q=D)
    wdqT_j3 = wdqT[:].rearrange("p (j q) -> p j q", q=FI)

    se16 = se[:].bitcast(F16)
    stbuf = [xb0, xb1]
    prev_use = {0: ("dv", v_ones), 1: ("pe", v_abpe)}
    item_idx = [0]
    prev_contrib = [0]

    def stage_load(src_ap, pw=D):
        p = item_idx[0] % 2
        item_idx[0] += 1
        buf = stbuf[p]
        ch = "ld" if p == 0 else "l1"
        v_l = dma(buf[:, :pw], src_ap, waits=(prev_use[p],), ch=ch)
        return buf, (ch, v_l), p

    def shard_item(srcrows, contrib_rows):
        buf, w_l, p = stage_load(srcrows)
        vq = dve(lambda e, b=buf: emit_quant_body(e, b[:], xq[:], D),
                 waits=(w_l, last_tp_evac[0]))
        prev_use[p] = ("dv", vq)
        wev = transpose_group(xq, [k * P for k in range(KD)], se16[:, 0:D],
                              pe_waits=(("dv", vq), ("st", prev_contrib[0])))
        vc = dma_out(contrib_rows, se16[:, 0:D], waits=(wev,))
        prev_contrib[0] = vc
        return vc

    for i in range(3):
        v_contrib_gu = shard_item(wus[i * P:(i + 1) * P, :],
                                  gu_in[i * P:(i + 1) * P, :])
    for i in range(3):
        v_contrib_gu = shard_item(wgs[i * P:(i + 1) * P, :],
                                  gu_in[(3 + i) * P:(4 + i) * P, :])
    for pc in range(3):
        p0 = pc * D
        pw = min(D, FI - p0)
        buf, w_l, p = stage_load(wds[:, p0:p0 + pw], pw)
        vq = dve(lambda e, b=buf, c=pw: emit_quant_body(e, b[:, 0:c],
                                                        xq[:, 0:c], c),
                 waits=(w_l, last_tp_evac[0]))
        prev_use[p] = ("dv", vq)
        wev = transpose_group(xq, [c * P for c in range(pw // P)],
                              hq[:, p0:p0 + pw], pe_waits=(("dv", vq),))
    v_contrib_wd = dma_out(wd_in.ap(), hq[:, 0:FI], waits=(wev,))

    v_cc = gp(lambda e: [
        e.wait_ge(sems["st"], v_contrib_gu),
        e.collective_compute("AllGather", ALU.bypass,
                             replica_groups=[list(range(8))],
                             ins=[gu_in.ap().opt()], outs=[gu_out.ap().opt()]
                             ).then_inc(sems["cc"], 1),
        e.wait_ge(sems["st"], v_contrib_wd),
        e.collective_compute("AllGather", ALU.bypass,
                             replica_groups=[list(range(8))],
                             ins=[wd_in.ap().opt()], outs=[wd_out.ap().opt()]
                             ).then_inc(sems["cc"], 1),
        e.wait_ge(sems["cc"], 2),
    ] + [e.dma_start(out=wdqT[:, j * FI:(j + 1) * FI],
                     in_=wd_out.ap()[j * P:(j + 1) * P, :]
                     ).then_inc(sems["lg"], 16) for j in range(KD)]
      + [e.wait_ge(sems["ac"], v_wT_exp)]
      + [e.dma_start(out=wgqT[:, j * D:(j + 1) * D],
                     in_=gu_out.ap()[((j // 3) * 6 + 3 + (j % 3)) * P:
                                     ((j // 3) * 6 + 4 + (j % 3)) * P, :]
                     ).then_inc(sems["lg"], 16) for j in range(22)])

    for j in range(KD):
        buf, w_l, p = stage_load(projd[j * P:(j + 1) * P, :])
        v_s = dve(lambda e, b=buf, jj=j: e.tensor_scalar(
            xq[:], b[:], colscr[:, CO_AB + jj:CO_AB + jj + 1], None, ALU.mult),
            waits=(w_l, ("dv", v_cols), last_tp_evac[0]))
        prev_use[p] = ("dv", v_s)
        transpose_group(xq, [k * P for k in range(KD)],
                        projST_3d[:, :, j * P:(j + 1) * P], pe_waits=(("dv", v_s),))

    prev_sq = 0
    for j in range(KD):
        buf, w_l, p = stage_load(iwd[j * P:(j + 1) * P, :])
        v_sq = act(lambda e, b=buf: [e.activation(se[:], b[:], AF.Square,
                                                  accum_out=acc_a), e.drain()],
                   waits=(w_l, ("dv", prev_sq),
                          ("st", v_contrib_gu)))
        prev_sq = dve(lambda e, jj=j: e.tensor_scalar(
            colscr[:, CO_SQW + jj:CO_SQW + jj + 1], acc_a, 0.5, None, ALU.mult),
            waits=(("ac", v_sq),))
        v_c = dve(lambda e, b=buf: e.tensor_copy(xq[:], b[:]),
                  waits=(("ac", v_sq), last_tp_evac[0]))
        prev_use[p] = ("dv", v_c)
        transpose_group(xq, [k * P for k in range(KD)],
                        iwpark_3d[:, 0:KD, j * P:(j + 1) * P], pe_waits=(("dv", v_c),))
    prev_consume = max(prev_use[0], prev_use[1], key=lambda w: w[1])

    v_sqwrow = col_to_row(colscr[:, CO_SQW:CO_SQW + KD], RO_B, D,
                          waits=(("dv", prev_sq),))

    v_wT_exp = 0
    for i in range(KD):
        for oc in range(2):
            def f_mm(e, ii=i, occ=oc):
                for k in range(KD):
                    e.matmul(ps_dot[:, 0:512],
                             projST_3d[:, k, ii * P:(ii + 1) * P],
                             iwpark_3d[:, k, occ * 512:(occ + 1) * 512],
                             start=(k == 0), stop=False)
                return e.matmul(ps_dot[:, 0:512],
                                rowsF[0:1, RO_NAB2 + ii * P:RO_NAB2 + (ii + 1) * P],
                                rowsF[0:1, RO_B + occ * 512:RO_B + (occ + 1) * 512],
                                start=False, stop=True)
            v_mm = pe(f_mm, waits=(("dv", max(v_sqwrow, v_nab2row)),
                                   prev_consume, last_tp_evac[0],
                                   ("ac", v_wT_exp)))
            v_wT_exp = act(lambda e, ii=i, occ=oc: e.activation(
                wT[:, ii * D + occ * 512: ii * D + (occ + 1) * 512],
                ps_dot[:, 0:512], AF.Exp,
                bias=colscr[:, CO_XLN + ii:CO_XLN + ii + 1]),
                waits=(("pe", v_mm),))

    v_x512row = 0
    for c0 in (0, 4):
        def f_x512pe(e, c0=c0):
            last = None
            for j in range(4):
                last = e.transpose(ps_dot[0:1, j * P:(j + 1) * P],
                                   colscr[:, CO_X512 + c0 + j:CO_X512 + c0 + j + 1],
                                   ident_f[:])
            return last
        v_x512pe = pe(f_x512pe, waits=(("ac", v_wT_exp), ("dv", v_sqwrow),
                                       ("dv", v_x512row)))
        v_x512row = dve(lambda e, c0=c0: [
            e.tensor_copy(rowsF[0:1, RO_B + c0 * P:RO_B + (c0 + 4) * P],
                          ps_dot[0:1, 0:512]),
            e.drain(),
            e.tensor_tensor(rowsF[0:1, RO_B2 + c0 * P:RO_B2 + (c0 + 4) * P],
                            ps_dot[0:1, 0:512],
                            rowsF[0:1, RO_B + c0 * P:RO_B + (c0 + 4) * P],
                            ALU.subtract),
        ], waits=(("pe", v_x512pe),))

    v_wload = 0
    for j in range(22):
        r, i = divmod(j, 3)
        v_wload = dma(wuqT[:, j * D:(j + 1) * D],
                      gu_out.ap()[(r * 6 + i) * P:(r * 6 + i + 1) * P, :],
                      waits=(("cc", 1),), ch="lw")


    st = [dict() for _ in range(NT + 1)]
    st[0]["att1"] = 0
    LCH = ["ld", "l1", "lx"]
    SCH = ["st", "s1", "s2"]
    vx = [0] * (NT + 3)
    vx[0] = dma(xb0[:], xd[0:P, :], waits=(("dv", v_x512row),), ch=LCH[0])
    vx[1] = dma(xb1[:], xd[P:2 * P, :], waits=(("dv", v_x512row),), ch=LCH[1])
    vx[2] = dma(xb2[:], xd[2 * P:3 * P, :], waits=(), ch=LCH[2])

    def g(t, key, default=0):
        return st[t][key] if t >= 0 and key in st[t] else default

    def emit_A_head(t):
        xb = xbuf[t % 3]
        s = st[t]
        s["cast"] = dve(lambda e, b=xb: e.tensor_copy(feats16[:], b[:]),
                        waits=((LCH[t % 3], vx[t]), ("pe", g(t - 1, "att1"))))
        v_sqp = dve(lambda e, b=xb: [
            e.scalar_tensor_tensor(hq[:, 0:D], b[:], 1.0, b[:], ALU.mult,
                                   ALU.mult, accum_out=acc_a),
            e.drain(),
            e.tensor_scalar(sqp_c, acc_a, 0.5, -512.0, ALU.mult, ALU.add),
        ], waits=((LCH[t % 3], vx[t]),))
        s["sqp"] = v_sqp
        v_sqT = pe(lambda e: e.transpose(ps_att[0:1, 0:P], sqp_c, ident_f[:]),
                   waits=(("dv", max(v_sqp, g(t - 1, "res"))),))
        s["corr"] = dve(lambda e: e.tensor_copy(rowsF[0:1, RO_SQ:RO_SQ + P],
                                                ps_att[0:1, 0:P]),
                        waits=(("pe", v_sqT), ("dv", g(t - 1, "corr"))))
        s["xT"] = transpose_group(feats16, [k * P for k in range(KD)],
                                  feats16[:, 0:D], pe_waits=(("dv", s["cast"]),))

    def emit_dot_h(t, b):
        s = st[t]

        def f_dot(e, bb=b, ps=(ps_dot if b == 0 else ps_att)):
            for k in range(KD):
                e.matmul(ps[:, 0:512],
                         feats16[:, k * P:(k + 1) * P],
                         projST_3d[:, k, bb * 512:(bb + 1) * 512],
                         start=(k == 0), stop=False)
            e.matmul(ps[:, 0:512],
                     rowsF[0:1, RO_SQ:RO_SQ + P],
                     rowsF[0:1, RO_NAB2 + bb * 512:RO_NAB2 + (bb + 1) * 512],
                     start=False, stop=False)
            e.matmul(ps[:, 0:512],
                     rowsF[0:1, RO_ONES:RO_ONES + P],
                     rowsF[0:1, RO_B + bb * 512:RO_B + (bb + 1) * 512],
                     start=False, stop=False)
            return e.matmul(ps[:, 0:512],
                            rowsF[0:1, RO_ONES:RO_ONES + P],
                            rowsF[0:1, RO_B2 + bb * 512:RO_B2 + (bb + 1) * 512],
                            start=False, stop=True)
        w = [s["xT"], ("dv", s["corr"])]
        if t == 0:
            w.append(("dv", v_x512row))
        if b == 0:
            w.append(("ac", g(t - 1, "exp", v_wT_exp)))
        else:
            w.append(("dv", g(t - 1, "res")))
        s[f"dot{b}"] = pe(f_dot, waits=tuple(w))
        if b == 1:
            s["exp"] = act(lambda e: [
                e.activation(feats[:, 0:512], ps_dot[:, 0:512], AF.Exp),
                e.activation(feats[:, 512:D], ps_att[:, 0:512], AF.Exp),
            ], waits=(("pe", s["dot1"]),))
            s["fT"] = transpose_group(feats, [i * P for i in range(KD)],
                                      feats[:, 0:D],
                                      pe_waits=(("ac", s["exp"]),), bf=True)

    def emit_att_h(t, b):
        s = st[t]

        def f_att(e, bb=b):
            last = None
            for i in range(KD):
                last = e.matmul(ps_att[:, 0:512],
                                feats[:, i * P:(i + 1) * P],
                                wT[:, i * D + bb * 512: i * D + (bb + 1) * 512],
                                start=(i == 0), stop=(i == KD - 1))
            return last
        xb = xbuf[t % 3]
        if b == 0:
            v = pe(f_att, waits=(s["fT"], ("ac", s["exp"])))
            s["add0"] = dve(lambda e, bx=xb: e.tensor_tensor(
                bx[:, 0:512], bx[:, 0:512], ps_att[:, 0:512], ALU.add),
                waits=(("pe", v),))
        else:
            v = pe(f_att, waits=(s["fT"], ("dv", s["add0"])))
            s["att1"] = v
            s["res"] = dve(lambda e, bx=xb: [
                e.tensor_tensor(bx[:, 512:D], bx[:, 512:D], ps_att[:, 0:512],
                                ALU.add),
                e.tensor_scalar(qa2f[:, 0:D], bx[:], 1.0, None, ALU.mult,
                                ALU.add, accum_out=LN_SLOTS[1][1]),
            ], waits=(("pe", v),))

    def emit_ln(which, v_ready, xbt):
        s_sq, s_sum, s_mean, s_var, s_rstd, s_nmb = LN_SLOTS[which]
        junk = se[:] if which == 1 else hq[:, 0:D]
        v_s2 = act(lambda e, j=junk: [e.activation(j, xbt[:], AF.Square,
                                                   accum_out=s_sq), e.drain()],
                   waits=(("dv", v_ready),))
        if which == 2:
            cur_ln2sq[0] = v_s2
        v_var = dve(lambda e: [
            e.scalar_tensor_tensor(s_var, s_sum, 1.0 / D, s_sum, ALU.mult, ALU.mult),
            e.tensor_scalar(s_mean, s_sum, -1.0 / D, None, ALU.mult),
            e.drain(),
            e.tensor_tensor(s_var, s_sq, s_var, ALU.subtract),
        ], waits=(("ac", v_s2),))
        return act(lambda e: [
            e.activation(s_rstd, s_var, AF.Ln, bias=eps_col, scale=1.0 / D),
            e.drain(),
            e.activation(s_rstd, s_rstd, AF.Exp, scale=-0.5),
            e.drain(),
            e.activation(s_nmb, s_mean, AF.Identity, scale=s_rstd),
            e.drain(),
            e.activation(xbt[:], xbt[:], AF.Identity, bias=s_nmb, scale=s_rstd),
        ], waits=(("dv", v_var),))

    cur_ln2sq = [0]

    def emit_ln1(t):
        s = st[t]
        s["x1"] = emit_ln(1, max(s["res"], g(t - 1, "hq2")), xbuf[t % 3])

    def emit_xq_head(t):
        xb = xbuf[t % 3]
        s = st[t]
        s["xqh"] = dve(lambda e, b=xb: emit_quant_head(e, b[:], D, se),
                       waits=(("ac", s["x1"]),))

    def emit_xq_tail(t):
        s = st[t]
        s["xq"] = dve(lambda e: emit_quant_tail(e, xq[:], D, dve_conv=True,
                                                vbuf=se),
                      waits=(("pe", g(t - 1, "mmv5")),))

    def emit_xqT(t):
        s = st[t]
        s["xqT"] = transpose_group(xq, [k * P for k in range(KD)], xq[:, 0:D],
                                   pe_waits=(("dv", s["xq"]),))

    CHUNKS = [(h * 512, min(512, FI - h * 512)) for h in range(6)]

    def emit_chunk(t, h):
        s = st[t]
        off, cw = CHUNKS[h]
        p = h % 2
        seo = p * 512
        nj = cw // P
        j0 = h * 4

        def f_mm_g(e, cw=cw, p=p, j0=j0, nj=nj):
            last = None
            for k in range(KD):
                last = e.matmul(ps_g[p][:, 0:cw], xq[:, k * P:(k + 1) * P],
                                wgqT_j3[:, j0:j0 + nj, k * P:(k + 1) * P],
                                start=(k == 0), stop=(k == KD - 1))
            return last

        def f_mm_u(e, cw=cw, p=p, j0=j0, nj=nj):
            last = None
            for k in range(KD):
                last = e.matmul(ps_u[p][:, 0:cw], xq[:, k * P:(k + 1) * P],
                                wuqT_j3[:, j0:j0 + nj, k * P:(k + 1) * P],
                                start=(k == 0), stop=(k == KD - 1))
            return last
        w = []
        wu = []
        if h == 0:
            w.append(s["xqT"])
        if h < 2:
            w.append(("dv", g(t - 1, "mulg%d" % (4 + h))))
            wu.append(("dv", g(t - 1, "mul%d" % (4 + h))))
            if t == 0:
                w.append(("lw", v_wload))
                w.append(("lg", 16 * (KD + 22)))
        else:
            w.append(("dv", s["mulg%d" % (h - 2)]))
            wu.append(("dv", s["mul%d" % (h - 2)]))
        v_mm = pe(f_mm_g, waits=tuple(w))
        v_mmu = pe(f_mm_u, waits=tuple(wu))
        s["mmv%d" % h] = v_mmu

        pair_prev = h // 2 - 1
        v_sig = act(lambda e, seo=seo, cw=cw, p=p: e.activation(
            se[:, seo:seo + cw], ps_g[p][:, 0:cw], AF.Tanh, scale=0.5),
            waits=(("pe", v_mm),
                   ("dv", s["hq%d" % pair_prev] if pair_prev >= 0
                    else s["xq"])))

        s["mulg%d" % h] = dve(lambda e, seo=seo, cw=cw, p=p:
                              e.scalar_tensor_tensor(
                                  se[:, seo:seo + cw], se[:, seo:seo + cw],
                                  1.0, ps_g[p][:, 0:cw], ALU.add, ALU.mult),
                              waits=(("ac", v_sig),))
        s["mul%d" % h] = dve(lambda e, seo=seo, cw=cw, p=p: e.tensor_tensor(
            se[:, seo:seo + cw], se[:, seo:seo + cw], ps_u[p][:, 0:cw],
            ALU.mult), waits=(("pe", v_mmu),))

    def emit_pair_qhA(t, pi):
        s = st[t]
        s["qa%d" % pi] = dve(lambda e: emit_quant_head(
            e, se[:, 0:512], 512, se, qsc_inv=1.0 / 12.0), waits=())

    def emit_pair_quant(t, pi):
        s = st[t]
        pw = 1024 if pi < 2 else 768
        s["qh%d" % pi] = dve(lambda e, c=pw - 512: emit_quant_head(
            e, se[:, 512:512 + c], c, se[:, 512:], nb_base=32,
            qsc_inv=1.0 / 12.0), waits=())
        s["hq%d" % pi] = dve(lambda e, o=1024 * pi, c=pw: emit_quant_tail(
            e, hq[:, o:o + c], c, dve_conv=True, vbuf=se),
            waits=(("pe", g(t - 1, "dno1") if pi == 0 else 0),
                   ("ac", g(t - 1, "ln2sq") if pi == 0 else 0),
                   ("lw", v_wload if (t == 0 and pi == 0) else 0),
                   ("lg", 16 * (KD + 22) if (t == 0 and pi == 0) else 0)))

    DOWN_GROUPS = [list(range(0, 8)), list(range(8, 16)), list(range(16, 22))]

    def emit_Thq_down(t, pgrp):
        s = st[t]
        cs = DOWN_GROUPS[pgrp]
        wev = transpose_group(hq, [c * P for c in cs],
                              hq[:, cs[0] * P:(cs[-1] + 1) * P],
                              pe_waits=(("dv", s["hq%d" % pgrp]),))

        def f_d(e, cs=tuple(cs), first=(pgrp == 0), lastg=(pgrp == 2)):
            last = None
            for i, c in enumerate(cs):
                last = e.matmul(ps_dn[:, 0:512], hq[:, c * P:(c + 1) * P],
                                wdqT_j3[:, 0:4, c * P:(c + 1) * P],
                                start=(first and i == 0),
                                stop=(lastg and i == len(cs) - 1))
            return last
        w = [wev]
        if pgrp == 0:
            w.append(("dv", g(t - 1, "res2")))
        v = pe(f_d, waits=tuple(w))
        if pgrp == 2:
            s["dno0"] = v
            s["ThqL"] = wev

    def emit_down_o1a(t):
        s = st[t]
        xb = xbuf[t % 3]
        s["res2h0"] = dve(lambda e, b=xb: e.tensor_tensor(
            b[:, 0:512], b[:, 0:512], ps_dn[:, 0:512], ALU.add),
            waits=(("pe", s["dno0"]),))

        def f_d1(e):
            last = None
            for c in range(KI):
                last = e.matmul(ps_dn[:, 0:512], hq[:, c * P:(c + 1) * P],
                                wdqT_j3[:, 4:8, c * P:(c + 1) * P],
                                start=(c == 0), stop=(c == KI - 1))
            return last
        s["dno1"] = pe(f_d1, waits=(("dv", s["res2h0"]), s["ThqL"]))

    def emit_down_o1b(t):
        s = st[t]
        xb = xbuf[t % 3]
        s["res2"] = dve(lambda e, b=xb: [
            e.tensor_tensor(b[:, 512:D], b[:, 512:D], ps_dn[:, 0:512], ALU.add),
            e.tensor_scalar(qa2f[:, 0:D], b[:], 1.0, None, ALU.mult, ALU.add,
                            accum_out=LN_SLOTS[2][1]),
        ], waits=(("pe", s["dno1"]),))

    def emit_ln2_store(t):
        xb = xbuf[t % 3]
        s = st[t]
        v_y = emit_ln(2, s["res2"], xb)
        s["ln2sq"] = cur_ln2sq[0]
        vs = dma_out(yd[t * P:(t + 1) * P, :], xb[:], waits=(("ac", v_y),),
                     ch=SCH[t % 3])
        if t + 3 < NT:
            vx[t + 3] = dma(xbuf[t % 3][:], xd[(t + 3) * P:(t + 4) * P, :],
                            waits=((SCH[t % 3], vs),), ch=LCH[t % 3])

    emit_A_head(0)
    emit_dot_h(0, 0)
    emit_dot_h(0, 1)
    emit_att_h(0, 0)
    emit_att_h(0, 1)
    emit_ln1(0)
    emit_xq_head(0)
    emit_xq_tail(0)
    for t in range(NT):
        emit_xqT(t)
        if t + 1 < NT:
            emit_A_head(t + 1)
        emit_chunk(t, 0)
        emit_pair_qhA(t, 0)
        emit_chunk(t, 1)
        if t + 1 < NT:
            emit_dot_h(t + 1, 0)
        emit_pair_quant(t, 0)
        emit_chunk(t, 2)
        emit_pair_qhA(t, 1)
        if t + 1 < NT:
            emit_dot_h(t + 1, 1)
        emit_chunk(t, 3)
        emit_pair_quant(t, 1)
        emit_Thq_down(t, 0)
        if t + 1 < NT:
            emit_att_h(t + 1, 0)
        emit_chunk(t, 4)
        emit_pair_qhA(t, 2)
        emit_chunk(t, 5)
        if t + 1 < NT:
            emit_att_h(t + 1, 1)
        emit_pair_quant(t, 2)
        emit_Thq_down(t, 1)
        if t + 1 < NT:
            emit_ln1(t + 1)
        emit_Thq_down(t, 2)
        if t + 1 < NT:
            emit_xq_head(t + 1)
        emit_down_o1a(t)
        if t + 1 < NT:
            emit_xq_tail(t + 1)
        emit_down_o1b(t)
        emit_ln2_store(t)

    tot = {c: cnt[c] for c in SCH}
    ops.append(("sync", lambda e: [e.wait_ge(sems[c], tot[c])
                                   for c in SCH if tot[c]]))

    block = es.enter_context(nc.Block())

    def replay(name):
        def f(e):
            for en, fn in ops:
                if en == name:
                    fn(e)
        return f
    block.sync(replay("sync"))
    block.vector(replay("vector"))
    block.scalar(replay("scalar"))
    block.tensor(replay("tensor"))
    block.gpsimd(replay("gpsimd"))
    es.close()
    return nc


_NC_CACHE = None


def kernel(L0_attn_x, L0_attn_input_weights, L0_attn_xis, L0_attn_proj_matrix,
           L0_ffn_w_gate, L0_ffn_w_up, L0_ffn_w_down):
    global _NC_CACHE
    x = np.ascontiguousarray(np.asarray(L0_attn_x, dtype=np.float32)).reshape(8 * TCORE, D)
    iw = np.ascontiguousarray(np.asarray(L0_attn_input_weights, dtype=np.float32))
    xis = np.ascontiguousarray(np.asarray(L0_attn_xis, dtype=np.float32)).reshape(1, D)
    proj = np.ascontiguousarray(np.asarray(L0_attn_proj_matrix, dtype=np.float32))
    wg = np.zeros((24 * P, D), dtype=np.float32)
    wg[:FI] = np.asarray(L0_ffn_w_gate, dtype=np.float32)
    wu = np.zeros((24 * P, D), dtype=np.float32)
    wu[:FI] = np.asarray(L0_ffn_w_up, dtype=np.float32)
    wd = np.ascontiguousarray(np.asarray(L0_ffn_w_down, dtype=np.float32))

    if _NC_CACHE is None:
        _NC_CACHE = build_core()
    nc = _NC_CACHE
    in_maps = []
    for c in range(8):
        in_maps.append({
            "x": x[c * TCORE:(c + 1) * TCORE],
            "iw": iw, "xis": xis, "proj": proj,
            "wgs": np.ascontiguousarray(wg[c * 3 * P:(c + 1) * 3 * P]),
            "wus": np.ascontiguousarray(wu[c * 3 * P:(c + 1) * 3 * P]),
            "wds": np.ascontiguousarray(wd[c * P:(c + 1) * P]),
        })
    res = run_bass_kernel_spmd(nc, in_maps, core_ids=list(range(8)))
    y = np.concatenate([res.results[c]["y"] for c in range(8)], axis=0)
    return y.reshape(8, TCORE, D).astype(np.float32)


if __name__ == "__main__":
    rng = np.random.RandomState(0)
    ins = {
        "L0_attn_x": rng.randn(8, TCORE, D).astype(np.float32),
        "L0_attn_input_weights": (rng.randn(D, D) * 0.03).astype(np.float32),
        "L0_attn_xis": (rng.rand(D) * 0.5).astype(np.float32),
        "L0_attn_proj_matrix": rng.randn(D, D).astype(np.float32),
        "L0_ffn_w_gate": (rng.randn(FI, D) * 0.02).astype(np.float32),
        "L0_ffn_w_up": (rng.randn(FI, D) * 0.02).astype(np.float32),
        "L0_ffn_w_down": (rng.randn(D, FI) * 0.02).astype(np.float32),
    }
    out = kernel(**ins)
    print("ok", out.shape, out.dtype, float(np.abs(out).mean()))
